# version 16
# baseline (speedup 1.0000x reference)
import numpy as np
from contextlib import ExitStack

import concourse.bass as bass
import concourse.mybir as mybir
from concourse.bass_utils import run_bass_kernel_spmd

F32 = mybir.dt.float32
BF16 = mybir.dt.bfloat16
AF = mybir.ActivationFunctionType
ALU = mybir.AluOpType

NCORE = 8
ENGINES = ("sp", "pe", "dve", "act", "pool")


class Prog:
    def __init__(self, nc, es):
        self.nc = nc
        self.es = es
        self.ops = {e: [] for e in ENGINES}
        self.sems = {}
        self.cnt = {}
        self.hwm = {e: {} for e in ENGINES}

    def sem(self, name):
        if name not in self.sems:
            self.sems[name] = self.es.enter_context(self.nc.semaphore(name))
            self.cnt[name] = 0
        return name

    SERIAL = ("dve", "act", "pool")

    def emit(self, eng, fn, waits=(), inc=None, is_dma=False):
        waits = list(waits)
        if fn is not None and eng in self.SERIAL and not is_dma:
            self.sem(eng)
            waits.append((eng, self.cnt[eng]))
            inc = (eng, 1)
        real = []
        for t in waits:
            if t is None:
                continue
            s, v = t
            if v <= 0 or self.hwm[eng].get(s, 0) >= v:
                continue
            self.hwm[eng][s] = v
            real.append((s, v))
        tok = None
        if inc is not None:
            s, d = inc
            self.sem(s)
            self.cnt[s] += d
            tok = (s, self.cnt[s])
        self.ops[eng].append((real, fn, inc))
        return tok

    def dma(self, eng, dst, src, slot, waits=(), **kw):
        self.sem(slot)
        throttle = (slot, self.cnt[slot])
        fn = lambda e, d=dst, s=src, kw_=dict(kw): e.dma_start(d, s, **kw_)
        return self.emit(eng, fn, waits=list(waits) + [throttle], inc=(slot, 16),
                         is_dma=True)

    def tok(self, sem):
        return (sem, self.cnt.get(sem, 0))

    def playback(self, block):
        def mk(name):
            def body(eng):
                for waits, fn, inc in self.ops[name]:
                    for s, v in waits:
                        eng.wait_ge(self.sems[s], v)
                    if fn is None:
                        continue
                    ins = fn(eng)
                    if inc is not None:
                        ins.then_inc(self.sems[inc[0]], inc[1])
            return body
        block.sync(mk("sp"))
        block.tensor(mk("pe"))
        block.vector(mk("dve"))
        block.scalar(mk("act"))
        block.gpsimd(mk("pool"))


def _lhsT(w):
    dout, din = w.shape
    return np.ascontiguousarray(w.T.reshape(din // 128, 128, dout))


def host_prepare(inputs, core, S=4, T=2048):
    import ml_dtypes
    I = inputs
    x = np.asarray(I["L0_attn_x"], np.float32)
    xs = x[core * S:(core + 1) * S]
    x_fm = np.ascontiguousarray(xs.transpose(0, 2, 1)).reshape(S, 8, 128, T)

    rope_q = np.asarray(I["L0_ffn_rope_q_w"], np.float32)
    rope_k = np.asarray(I["L0_ffn_rope_k_w"], np.float32)
    q_u = np.asarray(I["L0_ffn_q_u"], np.float32)
    k_u = np.asarray(I["L0_ffn_k_u"], np.float32)

    w1q, w2q, w1k, w2k = [], [], [], []
    for g in range(8):
        for h in (2 * g, 2 * g + 1):
            w1q += [rope_q[h * 32: h * 32 + 32], q_u[h * 64 + 32: h * 64 + 64]]
            w2q += [rope_q[h * 32 + 16: h * 32 + 32], rope_q[h * 32: h * 32 + 16],
                    np.zeros((32, 256), np.float32)]
            w1k += [rope_k[h * 32: h * 32 + 32], np.zeros((32, 1024), np.float32)]
            w2k += [rope_k[h * 32 + 16: h * 32 + 32], rope_k[h * 32: h * 32 + 16],
                    np.zeros((32, 1024), np.float32)]
    w1q = np.concatenate(w1q, 0); w2q = np.concatenate(w2q, 0)
    w1k = np.concatenate(w1k, 0); w2k = np.concatenate(w2k, 0)
    k_u_sel = np.concatenate([k_u[h * 64 + 32: h * 64 + 64] for h in range(16)], 0)

    gate_all = np.concatenate([np.asarray(I["L0_ffn_routed_gate"], np.float32),
                               np.asarray(I["L0_ffn_shared_gate"], np.float32)], 0)
    up_all = np.concatenate([np.asarray(I["L0_ffn_routed_up"], np.float32),
                             np.asarray(I["L0_ffn_shared_up"], np.float32)], 0)
    down_all = np.concatenate([np.asarray(I["L0_ffn_routed_down"], np.float32),
                               np.asarray(I["L0_ffn_shared_down"], np.float32)], 0)

    inv = (1.0 / (np.float32(10000.0) **
                  (np.arange(0, 32, 2, dtype=np.float32) / np.float32(32)))).astype(np.float32)
    t = np.arange(T, dtype=np.float32)
    fr = (t[None, :] * inv[:, None]).astype(np.float32)
    cos16 = np.cos(fr).astype(np.float32)
    sin16 = np.sin(fr).astype(np.float32)
    cosM = np.empty((128, T), np.float32)
    sinM = np.empty((128, T), np.float32)
    for blk in (0, 64):
        cosM[blk:blk + 16] = cos16; cosM[blk + 16:blk + 32] = cos16
        cosM[blk + 32:blk + 64] = 1.0
        sinM[blk:blk + 16] = sin16; sinM[blk + 16:blk + 32] = sin16
        sinM[blk + 32:blk + 64] = 0.0
    u = np.arange(896)[None, :] - 384
    pp = np.arange(128)[:, None]
    caus = np.where(u <= pp, np.float32(0.0), np.float32(-1e30)).astype(np.float32)

    m = {
        "x": x_fm,
        "w_in": _lhsT(np.asarray(I["L0_attn_input_w"], np.float32)),
        "w_gates": _lhsT(np.asarray(I["L0_attn_gates_w"], np.float32)),
        "w_out": _lhsT(np.asarray(I["L0_attn_output_w"], np.float32)),
        "w_grow": _lhsT(np.asarray(I["L0_attn_grow_w"], np.float32)),
        "w_shrink": _lhsT(np.asarray(I["L0_attn_shrink_w"], np.float32)),
        "w_kvd": _lhsT(np.asarray(I["L0_ffn_kv_d"], np.float32)),
        "w_qd": _lhsT(np.asarray(I["L0_ffn_q_d"], np.float32)),
        "w_vu": np.ascontiguousarray(np.asarray(I["L0_ffn_v_u"], np.float32).T.reshape(2, 128, 1024)),
        "w_q1": _lhsT(w1q), "w_q2": _lhsT(w2q),
        "w_k1": _lhsT(w1k), "w_k2": _lhsT(w2k),
        "w_kusel": np.ascontiguousarray(k_u_sel.T.reshape(2, 128, 512)),
        "w_kvd_l": np.ascontiguousarray(np.asarray(I["L0_ffn_kv_d"], np.float32).reshape(2, 128, 1024)),
        "w_o": _lhsT(np.asarray(I["L0_ffn_o_w"], np.float32)),
        "w_router": np.ascontiguousarray(np.asarray(I["L0_ffn_router_w"], np.float32).T.reshape(8, 128, 7)),
        "w_moe_g": np.stack([_lhsT(gate_all[e]) for e in range(8)]),
        "w_moe_u": np.stack([_lhsT(up_all[e]) for e in range(8)]),
        "w_moe_d": np.stack([_lhsT(down_all[e]) for e in range(8)]),
        "p_hawk_g": np.asarray(I["L0_attn_hawk_norm_gamma"], np.float32),
        "p_gmlp_g": np.asarray(I["L0_attn_gmlp_norm_gamma"], np.float32),
        "p_conv_w": np.ascontiguousarray(np.asarray(I["L0_attn_conv_w"], np.float32).reshape(1024, 4)),
        "p_conv_b": np.asarray(I["L0_attn_conv_b"], np.float32),
        "p_gates_b": np.asarray(I["L0_attn_gates_b"], np.float32),
        "p_fb": np.asarray(I["L0_attn_forget_base"], np.float32),
        "p_ln1": np.asarray(I["L0_ffn_ln1_w"], np.float32),
        "p_ln2": np.asarray(I["L0_ffn_ln2_w"], np.float32),
        "p_rbias": np.asarray(I["L0_ffn_routing_bias"], np.float32),
        "c_ident": np.eye(128).astype(ml_dtypes.bfloat16),
        "c_id32": np.eye(128).astype(np.float32),
        "c_sel8": _sel8(),
        "c_cos": cosM, "c_sin": sinM, "c_caus": caus,
    }
    return {k: np.ascontiguousarray(v) for k, v in m.items()}


def _sel8():
    s = np.zeros((8, 8 * 128), np.float32)
    for e in range(8):
        s[e, e * 128:(e + 1) * 128] = 1.0
    return s


class Cfg:
    def __init__(self, S=4, T=2048, stages=("A", "A2", "B1", "B2", "C"), dbg=()):
        self.S, self.T = S, T
        self.stages, self.dbg = tuple(stages), tuple(dbg)


class Builder:
    def __init__(self, cfg):
        self.cfg = cfg
        self.nc = bass.Bass("TRN2", target_bir_lowering=False)
        self.dr = {}
        self.wb = {}
        self.dbg = {}
        self.cast_done = {}

    def decl_dram(self):
        nc, cfg = self.nc, self.cfg
        S, T = cfg.S, cfg.T

        def din(name, shape, dt=F32):
            self.dr[name] = nc.dram_tensor(name, list(shape), dt, kind="ExternalInput").ap()
        din("x", (S, 8, 128, T))
        din("w_in", (8, 128, 2048)); din("w_gates", (8, 128, 2048)); din("w_out", (8, 128, 1024))
        din("w_grow", (8, 128, 4096)); din("w_shrink", (16, 128, 1024))
        din("w_kvd", (8, 128, 256)); din("w_qd", (8, 128, 256)); din("w_vu", (2, 128, 1024))
        din("w_q1", (2, 128, 1024)); din("w_q2", (2, 128, 1024))
        din("w_k1", (8, 128, 1024)); din("w_k2", (8, 128, 1024))
        din("w_kusel", (2, 128, 512)); din("w_kvd_l", (2, 128, 1024))
        din("w_o", (8, 128, 1024)); din("w_router", (8, 128, 7))
        din("w_moe_g", (8, 8, 128, 2048)); din("w_moe_u", (8, 8, 128, 2048))
        din("w_moe_d", (8, 16, 128, 1024))
        for nm in ("p_hawk_g", "p_gmlp_g", "p_conv_b", "p_fb", "p_ln1", "p_ln2"):
            din(nm, (1024,))
        din("p_conv_w", (1024, 4)); din("p_gates_b", (2048,)); din("p_rbias", (7,))
        din("c_ident", (128, 128), BF16)
        din("c_id32", (128, 128)); din("c_sel8", (8, 8 * 128))
        din("c_cos", (128, T)); din("c_sin", (128, T)); din("c_caus", (128, 896))

        self.out = nc.dram_tensor("out", [S, 8, 128, T], F32, kind="ExternalOutput").ap()
        for name in cfg.dbg:
            self.dbg[name] = nc.dram_tensor("dbg_" + name, [S, 8, 128, T], F32,
                                            kind="ExternalOutput").ap()

        def scr(name, shape, dt):
            return nc.dram_tensor(name, list(shape), dt, kind="Internal").ap()
        self.x1_fm = scr("x1_fm", (S, 8, 128, T), F32)
        self.y1_fm = scr("y1_fm", (S, 8, 128, T), F32)
        self.ain_d = scr("ain_d", (S, 8, 128, T), BF16)
        self.q_d = scr("q_dd", (S, 8, 128, T), BF16)
        self.k_dd = scr("k_dd", (S, 8, 128, T), BF16)
        self.v_dd = scr("v_dd", (S, T // 128, 128, 1024), BF16)
        self.min_d = scr("min_dd", (S, 8, 128, T), BF16)
        self.r_fm = scr("r_fm", (S, 8, 128, T), F32)
        self.wgt_d = scr("wgt_d", (max(1, S * T // 512), 4, 128, 8), F32)
        for nm, shape in [("w_in", (8, 128, 2048)), ("w_gates", (8, 128, 2048)),
                          ("w_out", (8, 128, 1024)), ("w_grow", (8, 128, 4096)),
                          ("w_shrink", (16, 128, 1024)), ("w_kvd", (8, 128, 256)),
                          ("w_qd", (8, 128, 256)), ("w_vu", (2, 128, 1024)),
                          ("w_q1", (2, 128, 1024)), ("w_q2", (2, 128, 1024)),
                          ("w_k1", (8, 128, 1024)), ("w_k2", (8, 128, 1024)),
                          ("w_o", (8, 128, 1024)),
                          ("w_moe_g", (8, 8, 128, 2048)), ("w_moe_u", (8, 8, 128, 2048)),
                          ("w_moe_d", (8, 16, 128, 1024))]:
            self.wb[nm] = scr(nm + "_bf", shape, BF16)

    def sb(self, name, shape, dt=F32):
        return self.es.enter_context(self.nc.sbuf_tensor(name, list(shape), dt))

    def psum(self, name, shape, dt=F32):
        return self.es.enter_context(self.nc.psum_tensor(name, list(shape), dt))

    @staticmethod
    def dfm(ap):
        return ap.rearrange("c p f -> p c f")

    @staticmethod
    def flatP(ap):
        nd = len(ap.shape)
        if nd == 2:
            return ap[:, None, :]
        if nd == 3:
            return ap.rearrange("a p l -> p a l")
        if nd == 4:
            return ap.rearrange("a b p l -> p (a b) l")
        raise ValueError(ap.shape)

    def tap(self, name, buf, tok, s, tt, FT):
        if name not in self.dbg:
            return
        t = self.p.dma("sp", self.dfm(self.dbg[name][s][:, :, tt * FT:(tt + 1) * FT]),
                       buf[:].rearrange("p (c f) -> p c f", f=FT), "s_tap_" + name,
                       waits=[tok])
        self.guard.append(t)
        return t

    def mk_ps(self, tag, row=True, bc=True, g=True):
        if row:
            self.ps_row = [self.psum(f"{tag}_row{i}", (1, 512)) for i in range(2)]
        if bc:
            self.ps_bc = [self.psum(f"{tag}_bc{i}", (128, 512)) for i in range(2)]
        if g:
            self.ps_g = [self.psum(f"{tag}_g{i}", (128, 512)) for i in range(2)]
        self.psg_free = [None, None]
        self.psrow_free = [None, None]
        self.psbc_free = [None, None]
        self.sq_free = None

    def stage_scope(self, slot_sems=None):
        p = self.p
        slots = [s for s in p.sems if s.startswith("s_")]
        toks = [p.tok("pe"), p.tok("dve"), p.tok("act"), p.tok("pool")] + [
            p.tok(s) for s in slots]
        for e in ("sp", "pe", "dve", "act", "pool"):
            for t in toks:
                p.emit(e, None, waits=[t])
        return toks

    def stage_prep(self):
        p, dr = self.p, self.dr
        sb = self.sb

        self.ones_b = sb("ones_b", (128, 1), BF16)
        self.ones_f = sb("ones_f", (1, 128), F32)
        self.gh = sb("gh", (128, 8))
        self.gg2 = sb("gg2", (128, 8))
        self.convw = sb("convw", (128, 32))
        self.convb = sb("convb", (128, 8))
        self.gatesb = sb("gatesb", (128, 16))
        self.acoef = sb("acoef", (128, 8))
        self.ln1c = sb("ln1c", (128, 8))
        self.ln2c = sb("ln2c", (128, 8))
        self.rbias = sb("rbias", (1, 7))
        self.fxev8 = sb("fxev8", (128, 8 * 512), BF16)

        def col_load(dst, name):
            src = dr[name].rearrange("(c p) -> p c", p=128)
            return p.dma("pool", dst[:, 0:src.shape[1]], src, "s_small",
                         allow_slow_non_contiguous=True)
        col_load(self.gh, "p_hawk_g")
        col_load(self.gg2, "p_gmlp_g")
        col_load(self.convb, "p_conv_b")
        col_load(self.acoef, "p_fb")
        col_load(self.ln1c, "p_ln1")
        col_load(self.ln2c, "p_ln2")
        p.dma("pool", self.convw[:].rearrange("p (c k) -> p c k", k=4),
              dr["p_conv_w"].rearrange("(c p) k -> p c k", p=128),
              "s_small", allow_slow_non_contiguous=True)
        p.dma("pool", self.gatesb[:], dr["p_gates_b"].rearrange("(c p) -> p c", p=128),
              "s_small", allow_slow_non_contiguous=True)
        p.dma("pool", self.rbias[:], dr["p_rbias"][None, :], "s_small")
        self.small_tok = p.tok("s_small")

        p.emit("dve", lambda e: e.memset(self.ones_b[:], 1.0))
        self.c1eps = self.sb("c1eps", (128, 1))
        p.emit("dve", lambda e: e.memset(self.c1eps[:], 1.0 + 1e-6))
        p.emit("dve", lambda e: e.memset(self.ones_f[:], 1.0), inc=("dve", 1))
        self.ones_tok = p.tok("dve")
        p.emit("act", lambda e: e.activation(self.acoef[:], self.acoef[:], AF.Exp),
               waits=[self.small_tok])
        tln = p.emit("act", lambda e: e.activation(self.acoef[:], self.acoef[:], AF.Ln,
                                                   bias=1.0), inc=("act", 1))
        p.emit("dve", lambda e: e.tensor_scalar_mul(self.acoef[:], self.acoef[:], -8.0),
               waits=[tln], inc=("dve", 1))
        self.acoef_tok = p.tok("dve")

        with ExitStack() as fes:
            old_es, self.es = self.es, fes
            kvdl32 = self.sb("kvdl32", (128, 2048))
            kusl32 = self.sb("kusl32", (128, 1024))
            kvdl = self.sb("kvdl", (128, 2048), BF16)
            kusl = self.sb("kusl", (128, 1024), BF16)
            psf = self.psum("psf", (128, 512))
            t1 = p.dma("pool", kvdl32[:].rearrange("p (a l) -> p a l", l=1024),
                       self.flatP(dr["w_kvd_l"]), "s_fx2")
            t2 = p.dma("pool", kusl32[:].rearrange("p (a l) -> p a l", l=512),
                       self.flatP(dr["w_kusel"]), "s_fx2")
            p.emit("pool", lambda e: e.tensor_copy(kvdl[:], kvdl32[:]), waits=[t1, t2])
            c2 = p.emit("pool", lambda e: e.tensor_copy(kusl[:], kusl32[:]), inc=("pool", 1))
            te = None
            for dchunk in range(8):
                tk = None
                for lc in range(2):
                    tk = p.emit("pe", lambda e, d=dchunk, lc=lc: e.matmul(
                        psf[:], kvdl[:, lc * 1024 + d * 128: lc * 1024 + (d + 1) * 128],
                        kusl[:, lc * 512:(lc + 1) * 512], start=(lc == 0), stop=(lc == 1)),
                        waits=([c2, te] if lc == 0 else ()),
                        inc=("pe", 1) if lc == 1 else None)
                te = p.emit("dve", lambda e, d=dchunk: e.tensor_copy(
                    self.fxev8[:, d * 512:(d + 1) * 512], psf[:]),
                    waits=[tk], inc=("dve", 1))
            self.fusedk_tok = te
            self.es = old_es

        CASTW = 512
        self.wst32 = [self.sb(f"wst32_{i}", (128, CASTW)) for i in range(2)]
        self.wst16 = [self.sb(f"wst16_{i}", (128, CASTW), BF16) for i in range(2)]
        gamma_fold = {"w_in": (self.gh, 32.0), "w_grow": (self.gg2, 32.0),
                      "w_q1": (None, 0.125), "w_q2": (None, 0.125)}
        plan = ["w_in", "w_gates", "w_out", "w_grow", "w_shrink",
                "w_q1", "w_q2", "w_kvd", "w_qd", "w_vu", "w_k1", "w_k2", "w_o",
                "w_moe_g", "w_moe_u", "w_moe_d"]
        for s in ("s_wl0", "s_wl1", "s_ws0", "s_ws1"):
            p.sem(s)
        slabs = []
        for nm in plan:
            src3 = self.flatP(self.dr[nm])
            dst3 = self.flatP(self.wb[nm])
            A, L = src3.shape[1], src3.shape[2]
            nkc = self.dr[nm].shape[0] if len(self.dr[nm].shape) == 3 else self.dr[nm].shape[1]
            fold = gamma_fold.get(nm)
            for a in range(A):
                for off in range(0, L, CASTW):
                    w = min(CASTW, L - off)
                    slabs.append((nm, src3, dst3, a, off, w, fold,
                                  (a % nkc) if fold is not None and fold[0] is not None else 0))
        n_slabs = len(slabs)
        cast_toks = [None] * n_slabs

        def emit_load(i):
            nm, src3, dst3, a, off, w, fold, kc = slabs[i]
            b = i % 2
            p.dma("pool", self.wst32[b][:, 0:w], src3[:, a, off:off + w],
                  f"s_wl{b}", waits=[cast_toks[i - 2] if i >= 2 else None,
                                     self.fusedk_tok])
        emit_load(0)
        if n_slabs > 1:
            emit_load(1)
        last_store = {}
        for i in range(n_slabs):
            nm, src3, dst3, a, off, w, fold, kc = slabs[i]
            b = i % 2
            gen = i // 2
            if fold is None:
                ceng = "pool"
                fn = lambda e, b=b, w=w: e.tensor_copy(self.wst16[b][:, 0:w],
                                                       self.wst32[b][:, 0:w])
            elif fold[0] is None:
                ceng = "dve"
                fn = lambda e, b=b, w=w, sc=fold[1]: e.tensor_scalar_mul(
                    self.wst16[b][:, 0:w], self.wst32[b][:, 0:w], sc)
            else:
                ceng = "dve"
                fn = lambda e, b=b, w=w, kc=kc, fold=fold: e.tensor_scalar(
                    self.wst16[b][:, 0:w], self.wst32[b][:, 0:w],
                    fold[0][:, kc:kc + 1], fold[1], ALU.mult, ALU.mult)
            cast_toks[i] = p.emit(ceng, fn,
                                  waits=[(f"s_wl{b}", 16 * (gen + 1)),
                                         (f"s_ws{b}", 16 * gen), self.small_tok])
            p.dma("pool", dst3[:, a, off:off + w], self.wst16[b][:, 0:w],
                  f"s_ws{b}", waits=[cast_toks[i]])
            if i + 2 < n_slabs:
                emit_load(i + 2)
            last_store[nm] = [p.tok("s_ws0"), p.tok("s_ws1")]
        for nm in plan:
            self.cast_done[nm] = last_store[nm]

        fix = self.sb("fixt", (128, 1024), BF16)
        for nm, kcs in (("w_q2", 2), ("w_k2", 8)):
            for c in range(kcs):
                tl = p.dma("sp", fix[:], self.wb[nm][c], "s_fix",
                           waits=self.cast_done[nm] + [self.fusedk_tok])
                v = fix[:].rearrange("p (g f) -> p g f", f=64)
                tfx = p.emit("dve",
                             lambda e, v=v: e.tensor_scalar_mul(v[:, :, 0:16],
                                                                v[:, :, 0:16], -1.0),
                             waits=[tl], inc=("dve", 1))
                p.dma("sp", self.wb[nm][c], fix[:], "s_fix", waits=[tfx])
        self.fix_done = p.tok("s_fix")

        for dchunk in range(8):
            dst = self.wb["w_k1"][dchunk].rearrange("p (g h x f) -> p g h x f",
                                                    h=2, x=2, f=32)[:, :, :, 1, :]
            src = self.fxev8[:, dchunk * 512:(dchunk + 1) * 512].rearrange(
                "p (g h f) -> p g h f", h=2, f=32)
            p.dma("sp", dst, src, "s_fx3",
                  waits=[self.fusedk_tok] + self.cast_done["w_k1"],
                  allow_slow_non_contiguous=True)
        self.k1_done = p.tok("s_fx3")

    def ln_fm(self, src, dst, gammac, par, FT, sqb, rows, mub, rsb, scr, in_tok,
              out_bf=False):
        p = self.p
        psr, psb = self.ps_row[par], self.ps_bc[par]
        tcv = p.emit("act", lambda e: e.copy(sqb[:], src[:]),
                     waits=[in_tok, self.sq_free], inc=("act", 1))
        tsum = None
        for kc in range(8):
            tsum = p.emit("pe", lambda e, kc=kc: e.matmul(
                psr[0:1, 0:FT], self.ones_b[:], sqb[:, kc * FT:(kc + 1) * FT],
                start=(kc == 0), stop=(kc == 7)),
                waits=([tcv, self.psrow_free[par], self.ones_tok] if kc == 0 else ()),
                inc=("pe", 1) if kc == 7 else None)
        tmu = p.emit("dve", lambda e: e.tensor_scalar_mul(rows[0:1, 0:FT],
                                                          psr[0:1, 0:FT], 1.0 / 1024),
                     waits=[tsum], inc=("dve", 1))
        tsq = p.emit("act", lambda e: e.activation(sqb[:], sqb[:], AF.Square),
                     waits=[tsum], inc=("act", 1))
        tss = None
        for kc in range(8):
            tss = p.emit("pe", lambda e, kc=kc: e.matmul(
                psr[0:1, 0:FT], self.ones_b[:], sqb[:, kc * FT:(kc + 1) * FT],
                start=(kc == 0), stop=(kc == 7)),
                waits=([tsq, tmu] if kc == 0 else ()),
                inc=("pe", 1) if kc == 7 else None)
        self.sq_free = tss
        tmu2 = p.emit("dve", lambda e: e.tensor_mul(rows[0:1, FT:2 * FT],
                                                    rows[0:1, 0:FT], rows[0:1, 0:FT]),
                      inc=("dve", 1))
        tvar = p.emit("dve", lambda e: e.scalar_tensor_tensor(
            rows[0:1, 2 * FT:3 * FT], psr[0:1, 0:FT], 1.0 / 1024,
            rows[0:1, FT:2 * FT], ALU.mult, ALU.subtract),
            waits=[tss, tmu2], inc=("dve", 1))
        self.psrow_free[par] = tvar
        trc = p.emit("dve", lambda e: e.tensor_scalar_add(
            rows[0:1, 2 * FT:3 * FT], rows[0:1, 2 * FT:3 * FT], 1e-5), inc=("dve", 1))
        tln_ = p.emit("act", lambda e: e.activation(rows[0:1, 2 * FT:3 * FT],
                                                    rows[0:1, 2 * FT:3 * FT], AF.Ln),
                      waits=[trc], inc=("act", 1))
        tsq3 = p.emit("act", lambda e: e.activation(rows[0:1, 2 * FT:3 * FT],
                                                    rows[0:1, 2 * FT:3 * FT], AF.Exp,
                                                    scale=-0.5),
                      inc=("act", 1))
        tb1 = p.emit("pe", lambda e: e.matmul(psb[:, 0:FT], self.ones_f[0:1, :],
                                              rows[0:1, 0:FT], start=True, stop=True),
                     waits=[tmu, self.psbc_free[par]], inc=("pe", 1))
        tc1 = p.emit("dve", lambda e: e.tensor_copy(mub[:], psb[:, 0:FT]),
                     waits=[tb1], inc=("dve", 1))
        tb2 = p.emit("pe", lambda e: e.matmul(psb[:, 0:FT], self.ones_f[0:1, :],
                                              rows[0:1, 2 * FT:3 * FT], start=True, stop=True),
                     waits=[tsq3, tc1], inc=("pe", 1))
        tc2 = p.emit("dve", lambda e: e.tensor_copy(rsb[:], psb[:, 0:FT]),
                     waits=[tb2], inc=("dve", 1))
        self.psbc_free[par] = tc2
        tlast = None
        for c in range(8):
            ts_ = p.emit("dve", lambda e, c=c: e.tensor_sub(
                scr[:], src[:, c * FT:(c + 1) * FT], mub[:]),
                waits=[tc1, tc2] if c == 0 else (), inc=("dve", 1))
            if gammac is not None:
                tlast = p.emit("dve", lambda e, c=c: e.scalar_tensor_tensor(
                    dst[:, c * FT:(c + 1) * FT], scr[:], gammac[:, c:c + 1], rsb[:],
                    ALU.mult, ALU.mult), inc=("dve", 1))
            else:
                tlast = p.emit("dve", lambda e, c=c: e.tensor_mul(
                    dst[:, c * FT:(c + 1) * FT], scr[:], rsb[:]), inc=("dve", 1))
        return tlast

    def stage_A1(self):
        cfg, p, dr = self.cfg, self.p, self.dr
        S, T = cfg.S, cfg.T
        FT = 256
        NTT = T // FT
        sb = self.sb

        self.mk_ps("A")
        ps_row, ps_bc = self.ps_row, self.ps_bc
        w_in = sb("A_w_in", (128, 8 * 2048), BF16)
        w_gt = sb("A_w_gt", (128, 8 * 2048), BF16)
        w_ot = sb("A_w_ot", (128, 8 * 1024), BF16)
        p.dma("sp", w_in[:].rearrange("p (k l) -> p k l", l=2048), self.wb["w_in"].rearrange("k p l -> p k l"),
              "s_Aw", waits=self.cast_done["w_in"])
        p.dma("sp", w_gt[:].rearrange("p (k l) -> p k l", l=2048), self.wb["w_gates"].rearrange("k p l -> p k l"),
              "s_Aw", waits=self.cast_done["w_gates"])
        p.dma("sp", w_ot[:].rearrange("p (k l) -> p k l", l=1024), self.wb["w_out"].rearrange("k p l -> p k l"),
              "s_Aw", waits=self.cast_done["w_out"])
        wA = p.tok("s_Aw")

        xb = [sb(f"A_xb{i}", (128, 8 * FT)) for i in range(3)]
        xbf = [sb(f"A_xbf{i}", (128, 8 * FT), BF16) for i in range(2)]
        sq = [sb(f"A_sq{i}", (128, 8 * FT), BF16) for i in range(2)]
        z0h = sb("A_z0h", (128, 8 * (FT + 3)))
        z0v = z0h[:].rearrange("p (c f) -> p c f", f=FT + 3)
        zb_ = sb("A_z", (128, 8 * FT))
        sig = sb("A_sig", (128, 8 * FT))
        alp = sb("A_alp", (128, 8 * FT))
        t1b = sb("A_t1", (128, 8 * FT))
        hbuf = sb("A_h", (128, 8 * FT))
        hst = sb("A_hst", (128, 8))
        zbf = sb("A_zbf", (128, 8 * FT), BF16)
        ggb = sb("A_ggb", (128, 8 * FT), BF16)
        hgb = [sb("A_hgb%d" % i, (128, 8 * FT), BF16) for i in range(2)]
        rrow = sb("A_rrow", (1, 2 * FT))
        rinvb = [sb(f"A_rinvb{i}", (128, 2 * FT)) for i in range(2)]

        psg = self.ps_g
        ps_o = self.psum("A_ps_o", (128, 512))
        psg_free = self.psg_free
        pso_free = None
        xfree = [None, None, None]
        xbf_free = [None, None]
        zstore_last = []
        pend = None
        sqfree = [None, None]
        trbt = {}

        def proj(wt, rhs_bf, rhs_tok, ngrp, kcs, kcw, evac):
            for grp in range(ngrp):
                gp = grp % 2
                tmm = None
                for dd in range(2):
                    d = grp * 2 + dd
                    for kc in range(kcs):
                        tmm = p.emit("pe", lambda e, gp=gp, dd=dd, d=d, kc=kc: e.matmul(
                            psg[gp][:, dd * FT:(dd + 1) * FT],
                            wt[:, kc * kcw + d * 128: kc * kcw + (d + 1) * 128],
                            rhs_bf[:, kc * FT:(kc + 1) * FT],
                            start=(kc == 0), stop=(kc == kcs - 1)),
                            waits=([rhs_tok, psg_free[gp], wA] if (kc == 0 and dd == 0) else ()),
                            inc=("pe", 1) if (dd == 1 and kc == kcs - 1) else None)
                psg_free[gp] = evac(grp, psg[gp], tmm)

        NTOT = S * NTT
        casts = {}

        def stage0(i):
            s0, tt0 = divmod(i, NTT)
            par0 = i % 2
            sl3 = i % 3
            xsrc = self.dfm(dr["x"][s0][:, :, tt0 * FT:(tt0 + 1) * FT])
            tload = p.dma("sp", xb[sl3][:].rearrange("p (c f) -> p c f", f=FT),
                          xsrc, f"s_Ax{sl3}", waits=[xfree[sl3]])
            self.tap("xin", xb[sl3], tload, s0, tt0, FT)
            p.emit("act", lambda e, par0=par0, sl3=sl3: e.copy(xbf[par0][:], xb[sl3][:]),
                   waits=[tload, xbf_free[par0]], inc=("act", 1))
            casts[i] = p.emit("act", lambda e, par0=par0: e.activation(
                sq[par0][:], xbf[par0][:], AF.Square),
                waits=[sqfree[par0]], inc=("act", 1))

        def emit_stats(j):
            parj = j % 2
            gtoks = self.guard
            self.guard = []
            tsums = None
            for kc in range(8):
                tsums = p.emit("pe", lambda e, kc=kc, parj=parj: e.matmul(
                    ps_row[parj][0:1, 0:FT], self.ones_b[:],
                    sq[parj][:, kc * FT:(kc + 1) * FT], start=(kc == 0), stop=(kc == 7)),
                    waits=([casts[j], self.psrow_free[parj], self.ones_tok, wA] if kc == 0 else ()),
                    inc=("pe", 1) if kc == 7 else None)
            sqfree[parj] = tsums
            trec = p.emit("act", lambda e, parj=parj: e.activation(
                rrow[0:1, 0:FT], ps_row[parj][0:1, 0:FT], AF.Ln),
                waits=[tsums] + gtoks, inc=("act", 1))
            self.psrow_free[parj] = trec
            tsq2 = p.emit("act", lambda e: e.activation(rrow[0:1, 0:FT],
                                                        rrow[0:1, 0:FT], AF.Exp,
                                                        scale=-0.5),
                          inc=("act", 1))
            tdup = p.emit("act", lambda e: e.copy(rrow[0:1, FT:2 * FT], rrow[0:1, 0:FT]),
                          inc=("act", 1))
            trbt[j] = tdup

        def emit_bcast(j):
            parj = j % 2
            tbc = p.emit("pe", lambda e, parj=parj: e.matmul(
                ps_bc[parj][:], self.ones_f[0:1, :], rrow[0:1, :],
                start=True, stop=True),
                waits=[trbt[j], self.psbc_free[parj]], inc=("pe", 1))
            trb = p.emit("dve", lambda e, parj=parj: e.tensor_copy(
                rinvb[parj][:], ps_bc[parj][:]), waits=[tbc], inc=("dve", 1))
            self.psbc_free[parj] = trb
            trbt[j] = trb

        stage0(0)
        emit_stats(0)
        emit_bcast(0)
        for s in range(S):
            for tt in range(NTT):
                i = s * NTT + tt
                par = i % 2
                if i + 1 < NTOT:
                    stage0(i + 1)
                gtoks = self.guard
                self.guard = []
                tcast = casts[i]
                trb = trbt[i]

                if tt > 0:
                    halo = p.emit("dve", lambda e: e.tensor_copy(
                        z0v[:, :, 0:3], z0v[:, :, FT:FT + 3]), inc=("dve", 1))
                else:
                    halo = p.emit("dve", lambda e: e.memset(z0v[:, :, 0:3], 0.0),
                                  inc=("dve", 1))

                conv_started = []
                def evac_u(grp, psv, tmm, par=par, trb=trb, halo=halo):
                    if grp < 4:
                        tv = p.emit("dve", lambda e, grp=grp, psv=psv: e.tensor_mul(
                            t1b[:, grp * 2 * FT:(grp + 1) * 2 * FT], psv[:], rinvb[par][:]),
                            waits=[tmm, trb], inc=("dve", 1))
                        p.emit("act", lambda e, grp=grp: e.activation(
                            ggb[:, grp * 2 * FT:(grp + 1) * 2 * FT],
                            t1b[:, grp * 2 * FT:(grp + 1) * 2 * FT], AF.Gelu),
                            waits=[tv], inc=("act", 1))
                        return tv
                    c0 = (grp - 4) * 2
                    tv = p.emit("dve", lambda e, c0=c0, psv=psv: e.tensor_mul(
                        z0v[:, c0:c0 + 2, 3:FT + 3],
                        psv[:].rearrange("p (c f) -> p c f", f=FT),
                        rinvb[par][:].rearrange("p (c f) -> p c f", f=FT)),
                        waits=[tmm, trb, halo], inc=("dve", 1))
                    for c in (c0, c0 + 1):
                        base = c * (FT + 3)
                        zv = zb_[:, c * FT:(c + 1) * FT]
                        p.emit("dve", lambda e, base=base, zv=zv, c=c: e.tensor_scalar(
                            zv, z0h[:, base + 3: base + 3 + FT],
                            self.convw[:, c * 4 + 3:c * 4 + 4],
                            self.convb[:, c:c + 1], ALU.mult, ALU.add),
                            waits=[self.small_tok] + zstore_last + gtoks)
                        for k in range(3):
                            tv = p.emit("dve", lambda e, base=base, zv=zv, c=c, k=k: e.scalar_tensor_tensor(
                                zv, z0h[:, base + k: base + k + FT],
                                self.convw[:, c * 4 + k:c * 4 + k + 1],
                                zv, ALU.mult, ALU.add))
                    conv_started.append(tv)
                    return tv
                proj(w_in, xbf[par], tcast, 8, 8, 2048, evac_u)
                tz0 = p.tok("dve")
                if "z0" in self.dbg:
                    tz0d = p.emit("dve", lambda e: e.tensor_copy(
                        t1b[:].rearrange("p (c f) -> p c f", f=FT),
                        z0v[:, :, 3:FT + 3]), waits=[tz0, p.tok("act")], inc=("dve", 1))
                    self.tap("z0", t1b, tz0d, s, tt, FT)
                tap_z0 = self.guard[-1] if "z0" in self.dbg else None
                xbf_free[par] = p.tok("pe")

                tconv = conv_started[-1]
                tzb = p.emit("act", lambda e: e.copy(zbf[:], zb_[:]),
                             waits=[tconv], inc=("act", 1))
                tap_zc = self.tap("zc", zb_, tconv, s, tt, FT)

                def evac_fg(grp, psv, tmm):
                    dstb = alp if grp < 4 else sig
                    coff = (grp % 4) * 2
                    tv = None
                    for dd in range(2):
                        d = grp * 2 + dd
                        c = coff + dd
                        tv = p.emit("act", lambda e, psv=psv, dd=dd, c=c, d=d, dstb=dstb: e.activation(
                            dstb[:, c * FT:(c + 1) * FT], psv[:, dd * FT:(dd + 1) * FT],
                            AF.Sigmoid, bias=self.gatesb[:, d:d + 1]),
                            waits=[tmm], inc=("act", 1))
                    return tv
                proj(w_gt, zbf, tzb, 8, 8, 2048, evac_fg)
                tsig = p.tok("act")

                ta1 = None
                for c in range(8):
                    ta1 = p.emit("dve", lambda e, c=c: e.tensor_scalar_mul(
                        alp[:, c * FT:(c + 1) * FT], alp[:, c * FT:(c + 1) * FT],
                        self.acoef[:, c:c + 1]),
                        waits=([tsig, self.acoef_tok] if c == 0 else ()),
                        inc=("dve", 1) if c == 7 else None)
                texp = p.emit("act", lambda e: e.activation(alp[:], alp[:], AF.Exp),
                              waits=[ta1], inc=("act", 1))
                self.tap("alp", alp, texp, s, tt, FT)
                self.tap("sig", sig, p.tok("act"), s, tt, FT)
                tt1 = p.emit("dve", lambda e: e.tensor_mul(t1b[:], alp[:], alp[:]),
                             waits=[texp, p.tok("act"), tap_z0], inc=("dve", 1))
                tt2a = p.emit("act", lambda e: e.activation(t1b[:], t1b[:], AF.Ln,
                                                            scale=-1.0, bias=self.c1eps[:]),
                              waits=[tt1], inc=("act", 1))
                tt2 = p.emit("act", lambda e: e.activation(t1b[:], t1b[:], AF.Exp,
                                                           scale=0.5),
                             inc=("act", 1))
                p.emit("dve", lambda e: e.tensor_mul(t1b[:], t1b[:], sig[:]),
                       waits=[tt2], inc=("dve", 1))
                tt4 = p.emit("dve", lambda e: e.tensor_mul(t1b[:], t1b[:], zb_[:]),
                             inc=("dve", 1))
                self.tap("xs", t1b, tt4, s, tt, FT)

                for c in range(8):
                    init = 0.0 if tt == 0 else hst[:, c:c + 1]
                    p.emit("dve", lambda e, c=c, init=init: e.tensor_tensor_scan(
                        hbuf[:, c * FT:(c + 1) * FT], alp[:, c * FT:(c + 1) * FT],
                        t1b[:, c * FT:(c + 1) * FT], init, ALU.mult, ALU.add),
                        waits=([tt4] if c == 0 else ()),
                        inc=("dve", 1) if c == 7 else None)
                p.emit("dve", lambda e: e.tensor_copy(
                    hst[:], hbuf[:].rearrange("p (c f) -> p c f", f=FT)[:, :, FT - 1:FT]),
                    inc=("dve", 1))
                thg = p.emit("dve", lambda e, par=par: e.tensor_mul(hgb[par][:], hbuf[:], ggb[:]),
                             inc=("dve", 1))
                self.tap("h", hbuf, p.tok("dve"), s, tt, FT)

                def stageB(ctx):
                    nonlocal pso_free, zstore_last
                    s_, tt_, par_, thg_ = ctx
                    sl3 = (s_ * NTT + tt_) % 3
                    tx1 = None
                    for d in range(8):
                        tmm = None
                        for kc in range(8):
                            tmm = p.emit("pe", lambda e, d=d, kc=kc, par_=par_: e.matmul(
                                ps_o[:, 0:FT],
                                w_ot[:, kc * 1024 + d * 128: kc * 1024 + (d + 1) * 128],
                                hgb[par_][:, kc * FT:(kc + 1) * FT],
                                start=(kc == 0), stop=(kc == 7)),
                                waits=([thg_, pso_free, wA] if kc == 0 else ()),
                                inc=("pe", 1) if kc == 7 else None)
                        pso_free = p.emit("dve", lambda e, d=d, sl3=sl3: e.tensor_add(
                            xb[sl3][:, d * FT:(d + 1) * FT], ps_o[:, 0:FT],
                            xb[sl3][:, d * FT:(d + 1) * FT]),
                            waits=[tmm] + zstore_last, inc=("dve", 1))
                    tx1 = pso_free
                    zst = p.dma("sp", self.dfm(self.x1_fm[s_][:, :, tt_ * FT:(tt_ + 1) * FT]),
                                xb[sl3][:].rearrange("p (c f) -> p c f", f=FT),
                                f"s_Az{par_}", waits=[tx1])
                    xfree[sl3] = zst
                    zstore_last = [zst]
                    if "x1" in self.dbg:
                        xfree[sl3] = p.dma(
                            "sp", self.dfm(self.dbg["x1"][s_][:, :, tt_ * FT:(tt_ + 1) * FT]),
                            xb[sl3][:].rearrange("p (c f) -> p c f", f=FT), "s_dbg",
                            waits=[tx1])
                        zstore_last = zstore_last + [xfree[sl3]]
                if i + 1 < NTOT:
                    emit_stats(i + 1)
                if pend is not None:
                    stageB(pend)
                pend = (s, tt, par, thg)
                if i + 1 < NTOT:
                    emit_bcast(i + 1)
        stageB(pend)
        self.a1_slots = ["s_Az0", "s_Az1", "s_Ax0", "s_Ax1", "s_Ax2", "s_Aw"]

    def stage_A2(self):
        cfg, p, dr = self.cfg, self.p, self.dr
        S, T = cfg.S, cfg.T
        FT = 256
        NTT = T // FT
        NTOT = S * NTT
        sb = self.sb

        self.mk_ps("G")
        ps_row, ps_bc = self.ps_row, self.ps_bc
        w_gr = sb("G_w_gr", (128, 8 * 4096), BF16)
        w_sh = sb("G_w_sh", (128, 16 * 1024), BF16)
        p.dma("sp", w_gr[:].rearrange("p (k l) -> p k l", l=4096),
              self.wb["w_grow"].rearrange("k p l -> p k l"), "s_Gw",
              waits=self.cast_done["w_grow"])
        p.dma("sp", w_sh[:].rearrange("p (k l) -> p k l", l=1024),
              self.wb["w_shrink"].rearrange("k p l -> p k l"), "s_Gw",
              waits=self.cast_done["w_shrink"])
        wG = p.tok("s_Gw")

        x1b = [sb("G_x1b%d" % i, (128, 8 * FT)) for i in range(2)]
        x1f = [sb("G_x1f%d" % i, (128, 8 * FT), BF16) for i in range(2)]
        xob = [sb("G_xob%d" % i, (128, 8 * FT)) for i in range(2)]
        sq = sb("G_sq", (128, 8 * FT), BF16)
        g2b = [sb("G_g2b%d" % i, (128, 16 * FT), BF16) for i in range(2)]
        tg = sb("G_tg", (128, FT))
        tv_ = sb("G_tv", (128, FT))
        sbs = sb("G_s", (128, 8 * FT))
        y1 = sb("G_y1", (128, 8 * FT))
        ainb = sb("G_ain", (128, 8 * FT), BF16)
        rrow = sb("G_rrow", (1, 2 * FT))
        rows3 = sb("G_rows3", (1, 3 * FT))
        rinvb = [sb("G_rinvb%d" % i, (128, 2 * FT)) for i in range(2)]
        mub = sb("G_mub", (128, FT))
        rsb = sb("G_rsb", (128, FT))
        scr = sb("G_scr", (128, FT))

        psg = self.ps_g
        psg_free = self.psg_free
        ps_o = self.psum("G_ps_o", (128, 512))
        pso_free = None
        x1free = [None, None]
        x1f_free = [None, None]
        xofree = [None, None]
        rinv_free = [None, None]
        casts = {}
        loadx = {}
        pend = None

        def stage0(i):
            s0, tt0 = divmod(i, NTT)
            par0 = i % 2
            tl1 = p.dma("sp", x1b[par0][:].rearrange("p (c f) -> p c f", f=FT),
                        self.dfm(self.x1_fm[s0][:, :, tt0 * FT:(tt0 + 1) * FT]),
                        "s_Gx1%d" % par0,
                        waits=[x1free[par0],
                               ("s_Az%d" % par0, 16 * (i // 2 + 1)) if "A" in cfg.stages else None])
            loadx[i] = p.dma("sp", xob[par0][:].rearrange("p (c f) -> p c f", f=FT),
                             self.dfm(dr["x"][s0][:, :, tt0 * FT:(tt0 + 1) * FT]),
                             "s_Gx%d" % par0, waits=[xofree[par0]])
            p.emit("act", lambda e, par0=par0: e.copy(x1f[par0][:], x1b[par0][:]),
                   waits=[tl1, x1f_free[par0]], inc=("act", 1))
            casts[i] = p.emit("act", lambda e, par0=par0: e.activation(
                sq[:], x1f[par0][:], AF.Square),
                waits=[self.sq_free], inc=("act", 1))

        def stageB(ctx):
            nonlocal pso_free
            i_, s_, tt_, par_, tg2_ = ctx
            tlast = None
            for d in range(8):
                tmm = None
                for kc in range(16):
                    tmm = p.emit("pe", lambda e, d=d, kc=kc, par_=par_: e.matmul(
                        ps_o[:, 0:FT],
                        w_sh[:, kc * 1024 + d * 128: kc * 1024 + (d + 1) * 128],
                        g2b[par_][:, kc * FT:(kc + 1) * FT],
                        start=(kc == 0), stop=(kc == 15)),
                        waits=([tg2_, pso_free, wG] if kc == 0 else ()),
                        inc=("pe", 1) if (kc == 15) else None)
                t_ = p.emit("dve", lambda e, d=d, par_=par_: e.tensor_add(
                    sbs[:, d * FT:(d + 1) * FT], ps_o[:, 0:FT],
                    x1b[par_][:, d * FT:(d + 1) * FT]),
                    waits=[tmm] + self.guard, inc=("dve", 1))
                self.guard = []
                tlast = p.emit("dve", lambda e, d=d, par_=par_: e.tensor_add(
                    sbs[:, d * FT:(d + 1) * FT], sbs[:, d * FT:(d + 1) * FT],
                    xob[par_][:, d * FT:(d + 1) * FT]),
                    waits=[loadx[i_]], inc=("dve", 1))
                pso_free = tlast
            x1free[par_] = tlast
            xofree[par_] = tlast

            ty1 = self.ln_fm(sbs, y1, None, par_, FT, sq, rows3, mub, rsb, scr,
                             tlast)
            t1 = p.dma("sp", self.dfm(self.y1_fm[s_][:, :, tt_ * FT:(tt_ + 1) * FT]),
                       y1[:].rearrange("p (c f) -> p c f", f=FT), "s_Gy%d" % par_,
                       waits=[ty1])
            self.guard.append(t1)
            if "y1" in self.dbg:
                t_ = p.dma("sp", self.dfm(self.dbg["y1"][s_][:, :, tt_ * FT:(tt_ + 1) * FT]),
                           y1[:].rearrange("p (c f) -> p c f", f=FT), "s_dbg2",
                           waits=[ty1])
                self.guard.append(t_)
            tain = self.ln_fm(y1, ainb, self.ln1c, par_, FT, sq, rows3, mub, rsb,
                              scr, ty1)
            t2 = p.dma("sp", self.dfm(self.ain_d[s_][:, :, tt_ * FT:(tt_ + 1) * FT]),
                       ainb[:].rearrange("p (c f) -> p c f", f=FT), "s_Ga%d" % par_,
                       waits=[tain])
            self.guard.append(t2)
            if "ain" in self.dbg:
                tup = p.emit("dve", lambda e: e.tensor_copy(sbs[:], ainb[:]),
                             waits=[tain], inc=("dve", 1))
                t_ = p.dma("sp", self.dfm(self.dbg["ain"][s_][:, :, tt_ * FT:(tt_ + 1) * FT]),
                           sbs[:].rearrange("p (c f) -> p c f", f=FT), "s_dbg3",
                           waits=[tup])
                self.guard.append(t_)

        stage0(0)
        for s in range(S):
            for tt in range(NTT):
                i = s * NTT + tt
                par = i % 2
                tc1 = casts[i]
                tsq = tc1
                tsums = None
                for kc in range(8):
                    tsums = p.emit("pe", lambda e, kc=kc, par=par: e.matmul(
                        ps_row[par][0:1, 0:FT], self.ones_b[:],
                        sq[:, kc * FT:(kc + 1) * FT], start=(kc == 0), stop=(kc == 7)),
                        waits=([tsq, self.psrow_free[par], wG] if kc == 0 else ()),
                        inc=("pe", 1) if kc == 7 else None)
                self.sq_free = tsums
                trec = p.emit("act", lambda e, par=par: e.activation(
                    rrow[0:1, 0:FT], ps_row[par][0:1, 0:FT], AF.Ln),
                    waits=[tsums], inc=("act", 1))
                self.psrow_free[par] = trec
                p.emit("act", lambda e: e.activation(rrow[0:1, 0:FT], rrow[0:1, 0:FT],
                                                     AF.Exp, scale=-0.5),
                       inc=("act", 1))
                tdup = p.emit("act", lambda e: e.copy(rrow[0:1, FT:2 * FT], rrow[0:1, 0:FT]),
                              inc=("act", 1))
                tbc = p.emit("pe", lambda e, par=par: e.matmul(
                    ps_bc[par][:], self.ones_f[0:1, :], rrow[0:1, :],
                    start=True, stop=True),
                    waits=[tdup, self.psbc_free[par]], inc=("pe", 1))
                trb = p.emit("dve", lambda e, par=par: e.tensor_copy(
                    rinvb[par][:], ps_bc[par][:]), waits=[tbc], inc=("dve", 1))
                self.psbc_free[par] = trb

                tg2 = None
                for c in range(16):
                    gp = c % 2
                    tmm = None
                    for dd, d in enumerate((c, 16 + c)):
                        for kc in range(8):
                            tmm = p.emit("pe", lambda e, gp=gp, dd=dd, d=d, kc=kc, par=par: e.matmul(
                                psg[gp][:, dd * FT:(dd + 1) * FT],
                                w_gr[:, kc * 4096 + d * 128: kc * 4096 + (d + 1) * 128],
                                x1f[par][:, kc * FT:(kc + 1) * FT],
                                start=(kc == 0), stop=(kc == 7)),
                                waits=([tc1, psg_free[gp], wG] if (kc == 0 and dd == 0) else ()),
                                inc=("pe", 1) if (dd == 1 and kc == 7) else None)
                    ta = p.emit("dve", lambda e, gp=gp, par=par: e.tensor_mul(
                        tg[:], psg[gp][:, 0:FT], rinvb[par][:, 0:FT]),
                        waits=[tmm, trb], inc=("dve", 1))
                    tb = p.emit("dve", lambda e, gp=gp, par=par: e.tensor_mul(
                        tv_[:], psg[gp][:, FT:2 * FT], rinvb[par][:, 0:FT]),
                        inc=("dve", 1))
                    tgl = p.emit("act", lambda e: e.activation(tg[:], tg[:], AF.Gelu),
                                 waits=[ta], inc=("act", 1))
                    tg2 = p.emit("dve", lambda e, c=c, par=par: e.tensor_mul(
                        g2b[par][:, c * FT:(c + 1) * FT], tg[:], tv_[:]),
                        waits=[tgl], inc=("dve", 1))
                    psg_free[gp] = tg2
                x1f_free[par] = p.tok("pe")
                rinv_free[par] = p.tok("dve")

                if pend is not None:
                    stageB(pend)
                pend = (i, s, tt, par, tg2)
                if i + 1 < NTOT:
                    stage0(i + 1)
        stageB(pend)
        self.a2_slots = ["s_Gx10", "s_Gx11", "s_Gx0", "s_Gx1", "s_Gy0", "s_Gy1",
                         "s_Ga0", "s_Ga1", "s_Gw"]

    def stage_B1(self):
        cfg, p, dr = self.cfg, self.p, self.dr
        S, T = cfg.S, cfg.T
        FT = 512
        NTT = T // FT
        sb = self.sb

        self.mk_ps("M", row=False, bc=False)
        wq1 = sb("M_wq1", (128, 2 * 1024), BF16)
        wq2 = sb("M_wq2", (128, 2 * 1024), BF16)
        wk1 = sb("M_wk1", (128, 8 * 1024), BF16)
        wk2 = sb("M_wk2", (128, 8 * 1024), BF16)
        wkvd = sb("M_wkvd", (128, 8 * 256), BF16)
        wqd = sb("M_wqd", (128, 8 * 256), BF16)
        wvu = sb("M_wvu", (128, 2 * 1024), BF16)
        cosM = sb("M_cos", (128, T))
        sinM = sb("M_sin", (128, T))
        p.dma("sp", wq1[:].rearrange("p (k l) -> p k l", l=1024),
              self.wb["w_q1"].rearrange("k p l -> p k l"), "s_Mw",
              waits=self.cast_done["w_q1"])
        p.dma("sp", wq2[:].rearrange("p (k l) -> p k l", l=1024),
              self.wb["w_q2"].rearrange("k p l -> p k l"), "s_Mw",
              waits=self.cast_done["w_q2"] + [self.fix_done])
        p.dma("sp", wk1[:].rearrange("p (k l) -> p k l", l=1024),
              self.wb["w_k1"].rearrange("k p l -> p k l"), "s_Mw",
              waits=self.cast_done["w_k1"] + [self.k1_done])
        p.dma("sp", wk2[:].rearrange("p (k l) -> p k l", l=1024),
              self.wb["w_k2"].rearrange("k p l -> p k l"), "s_Mw",
              waits=self.cast_done["w_k2"] + [self.fix_done])
        p.dma("sp", wkvd[:].rearrange("p (k l) -> p k l", l=256),
              self.wb["w_kvd"].rearrange("k p l -> p k l"), "s_Mw",
              waits=self.cast_done["w_kvd"])
        p.dma("sp", wqd[:].rearrange("p (k l) -> p k l", l=256),
              self.wb["w_qd"].rearrange("k p l -> p k l"), "s_Mw",
              waits=self.cast_done["w_qd"])
        p.dma("sp", wvu[:].rearrange("p (k l) -> p k l", l=1024),
              self.wb["w_vu"].rearrange("k p l -> p k l"), "s_Mw",
              waits=self.cast_done["w_vu"])
        p.dma("sp", cosM[:], dr["c_cos"], "s_Mw")
        p.dma("sp", sinM[:], dr["c_sin"], "s_Mw")
        wM = p.tok("s_Mw")

        ainb = [sb("M_ain%d" % i, (128, 8 * FT), BF16) for i in range(2)]
        kvl = sb("M_kvl", (128, 2 * FT), BF16)
        ql = sb("M_ql", (128, 2 * FT), BF16)
        scr1 = sb("M_scr1", (128, FT))
        scr2 = sb("M_scr2", (128, FT))
        qkout = sb("M_qk", (128, 8 * FT), BF16)
        vout = sb("M_vout", (128, 1024), BF16)

        psg = self.ps_g
        psg_free = self.psg_free
        ainfree = [None, None]
        qkout_free = [None]
        vfree = [None]
        latfree = [None]

        for s in range(S):
            for tt in range(NTT):
                i = s * NTT + tt
                par = i % 2
                tla = p.dma("sp", ainb[par][:].rearrange("p (c f) -> p c f", f=FT),
                            self.dfm(self.ain_d[s][:, :, tt * FT:(tt + 1) * FT]),
                            "s_Ma%d" % par, waits=[ainfree[par]])

                def lat(wt, dstb, rhs_tok):
                    tlast = None
                    for d in range(2):
                        gp = d % 2
                        tmm = None
                        for kc in range(8):
                            tmm = p.emit("pe", lambda e, gp=gp, d=d, kc=kc, par=par, wt=wt: e.matmul(
                                psg[gp][:, 0:FT],
                                wt[:, kc * 256 + d * 128: kc * 256 + (d + 1) * 128],
                                ainb[par][:, kc * FT:(kc + 1) * FT],
                                start=(kc == 0), stop=(kc == 7)),
                                waits=([rhs_tok, psg_free[gp], wM] if kc == 0 else ()),
                                inc=("pe", 1) if kc == 7 else None)
                        tlast = p.emit("dve", lambda e, gp=gp, d=d, dstb=dstb: e.tensor_copy(
                            dstb[:, d * FT:(d + 1) * FT], psg[gp][:, 0:FT]),
                            waits=[tmm, latfree[0]], inc=("dve", 1))
                        psg_free[gp] = tlast
                    return tlast
                tkv = lat(wkvd, kvl, tla)
                tql = lat(wqd, ql, tla)

                def qk_tiles(w1, w2, rhsb, nkc, rhs_tok, dst_d, slot, extra_w):
                    for g in range(8):
                        tm1 = tm2 = None
                        for kc in range(nkc):
                            tm1 = p.emit("pe", lambda e, g=g, kc=kc, w1=w1, rhsb=rhsb: e.matmul(
                                psg[0][:, 0:FT],
                                w1[:, kc * 1024 + g * 128: kc * 1024 + (g + 1) * 128],
                                rhsb[:, kc * FT:(kc + 1) * FT],
                                start=(kc == 0), stop=(kc == nkc - 1)),
                                waits=([rhs_tok, psg_free[0], wM] if kc == 0 else ()),
                                inc=("pe", 1) if kc == nkc - 1 else None)
                        for kc in range(nkc):
                            tm2 = p.emit("pe", lambda e, g=g, kc=kc, w2=w2, rhsb=rhsb: e.matmul(
                                psg[1][:, 0:FT],
                                w2[:, kc * 1024 + g * 128: kc * 1024 + (g + 1) * 128],
                                rhsb[:, kc * FT:(kc + 1) * FT],
                                start=(kc == 0), stop=(kc == nkc - 1)),
                                waits=([psg_free[1]] if kc == 0 else ()),
                                inc=("pe", 1) if kc == nkc - 1 else None)
                        t1 = p.emit("dve", lambda e, tt=tt: e.tensor_mul(
                            scr1[:], psg[0][:, 0:FT], cosM[:, tt * FT:(tt + 1) * FT]),
                            waits=[tm1, wM] + (extra_w if g == 0 else []),
                            inc=("dve", 1))
                        psg_free[0] = t1
                        t2 = p.emit("dve", lambda e, tt=tt: e.tensor_mul(
                            scr2[:], psg[1][:, 0:FT], sinM[:, tt * FT:(tt + 1) * FT]),
                            waits=[tm2], inc=("dve", 1))
                        psg_free[1] = t2
                        p.emit("dve", lambda e, g=g: e.tensor_add(
                            qkout[:, g * FT:(g + 1) * FT], scr1[:], scr2[:]),
                            inc=("dve", 1))
                    tq = p.tok("dve")
                    return p.dma("sp", self.dfm(dst_d[s][:, :, tt * FT:(tt + 1) * FT]),
                                 qkout[:].rearrange("p (c f) -> p c f", f=FT),
                                 slot, waits=[tq])
                tqs = qk_tiles(wq1, wq2, ql, 2, tql, self.q_d, "s_MqS%d" % par,
                               qkout_free)
                tks = qk_tiles(wk1, wk2, ainb[par], 8, tla, self.k_dd, "s_MkS%d" % par,
                               [tqs])
                qkout_free = [tks]

                for tk4 in range(FT // 128):
                    tmm = None
                    for half in range(2):
                        for lc in range(2):
                            tmm = p.emit("pe", lambda e, tk4=tk4, half=half, lc=lc: e.matmul(
                                psg[half][:, 0:FT],
                                kvl[:, lc * FT + tk4 * 128: lc * FT + tk4 * 128 + 128],
                                wvu[:, lc * 1024 + half * 512: lc * 1024 + (half + 1) * 512],
                                start=(lc == 0), stop=(lc == 1)),
                                waits=([tkv, psg_free[half]] if lc == 0 else ()),
                                inc=("pe", 1) if lc == 1 else None)
                    tvc = p.emit("dve", lambda e: e.tensor_copy(vout[:, 0:512], psg[0][:, 0:FT]),
                                 waits=[tmm, vfree[0]], inc=("dve", 1))
                    psg_free[0] = tvc
                    tvc2 = p.emit("dve", lambda e: e.tensor_copy(vout[:, 512:1024], psg[1][:, 0:FT]),
                                  inc=("dve", 1))
                    psg_free[1] = tvc2
                    tv = p.dma("sp", self.v_dd[s][tt * 4 + tk4], vout[:],
                               "s_Mv%d" % par, waits=[tvc2])
                    vfree = [tv]
                ainfree[par] = p.tok("pe")
                latfree = [p.tok("pe")]

    def stage_B2(self):
        cfg, p, dr = self.cfg, self.p, self.dr
        S, T = cfg.S, cfg.T
        FT = 512
        NQT = T // FT
        sb = self.sb

        self.ps_row = [self.psum("T_row", (1, 512))]
        self.ps_bc = [self.psum("T_bc", (128, 512))]
        self.psrow_free = [None]
        self.psbc_free = [None]
        self.sq_free = None
        ps_s = self.psum("T_ps_s", (128, 2048))
        ps_t = self.psum("T_ps_t", (128, 512), BF16)
        ps_pv = self.psum("T_ps_pv", (128, 512))

        w_o = sb("T_wo", (128, 8 * 1024), BF16)
        caus = sb("T_caus", (128, 896))
        ident = sb("T_id", (128, 128), BF16)
        p.dma("sp", w_o[:].rearrange("p (k l) -> p k l", l=1024),
              self.wb["w_o"].rearrange("k p l -> p k l"), "s_Tw",
              waits=self.cast_done["w_o"])
        p.dma("sp", caus[:], dr["c_caus"], "s_Tw")
        p.dma("sp", ident[:], dr["c_ident"], "s_Tw")
        wT = p.tok("s_Tw")

        Ks = sb("T_K", (128, 8 * T), BF16)
        Vs = sb("T_V", (128, 16 * 1024), BF16)
        Q1 = sb("T_Q", (128, 8 * FT), BF16)
        Qs = [Q1, Q1]
        y1b_ = sb("T_y1", (128, 8 * FT))
        y1b = [y1b_, y1b_]
        Abf = sb("T_Abf", (128, 2048), BF16)
        AT = sb("T_AT", (128, 16 * 512), BF16)
        obf = sb("T_obf", (128, 8 * FT), BF16)
        y2 = sb("T_y2", (128, 8 * FT))
        rbuf = y1b_
        minb = AT[:, 8 * FT:16 * FT]
        sqb = AT[:, 0:8 * FT]
        sc = [sb("T_sc%d" % i, (128, 2048)) for i in range(3)]
        rows3 = sc[1][0:1, 0:3 * FT]
        Ab2 = [Abf, sb("T_Abf2", (128, 2048), BF16), sb("T_Abf3", (128, 2048), BF16)]
        mub = sc[0][:, 0:FT]
        rsb = sc[0][:, FT:2 * FT]
        scr = sc[0][:, 2 * FT:3 * FT]
        nmax = [sb("T_nmax%d" % i, (128, 1)) for i in range(3)]
        rsum = [sb("T_rsum%d" % i, (128, 1)) for i in range(3)]

        qfree = [None, None]
        ps_free = {}
        sc_free = [[], [], []]
        nmax_free = [None, None, None]
        abf_free2 = [None, None, None]
        ps_t_free = None
        ps_pv_free = None
        at_free = []
        o_free = None
        abf_free = None
        kv_free = None
        y2_free = None
        min_store = None
        r_store = None

        for s in range(S):
            kload = p.dma("sp", Ks[:].rearrange("p (c f) -> p c f", f=T),
                          self.dfm(self.k_dd[s]), "s_TK", waits=[kv_free])
            p.dma("sp", Vs[:, 0:(T // 128) * 1024].rearrange("p (c f) -> p c f", f=1024),
                  self.v_dd[s].rearrange("c p f -> p c f"), "s_TK")
            kv_tok = p.tok("s_TK")
            for qt in range(NQT):
                i = s * NQT + qt
                par = i % 2
                tlq = p.dma("sp", Q1[:].rearrange("p (c f) -> p c f", f=FT),
                            self.dfm(self.q_d[s][:, :, qt * FT:(qt + 1) * FT]),
                            "s_TQ", waits=[qfree[0]])
                tly = p.dma("sp", y1b_[:].rearrange("p (c f) -> p c f", f=FT),
                            self.dfm(self.y1_fm[s][:, :, qt * FT:(qt + 1) * FT]),
                            "s_Ty", waits=[r_store])
                kts = qt + 1
                W = kts * 512
                def do_scores(h, qj):
                    idx = h * 4 + qj
                    g, hh = h // 2, h % 2
                    st = idx % 3
                    scs = sc[st]
                    tmm = {}
                    for kt in range(kts):
                        tmm[kt] = p.emit("pe", lambda e, g=g, hh=hh, qj=qj, kt=kt, par=par: e.matmul(
                            ps_s[:, kt * 512:(kt + 1) * 512],
                            Qs[par][hh * 64:(hh + 1) * 64,
                                    g * FT + qj * 128: g * FT + qj * 128 + 128],
                            Ks[hh * 64:(hh + 1) * 64,
                               g * T + kt * 512: g * T + (kt + 1) * 512],
                            start=True, stop=True),
                            waits=([tlq, kv_tok, wT, ps_free.get(kt)]
                                   + (sc_free[st] if kt == 0 else [])),
                            inc=("pe", 1))
                    delta = qj * 128
                    evs = []
                    for kt in range(kts):
                        if kt == kts - 1:
                            tev = p.emit("dve", lambda e, kt=kt, delta=delta, scs=scs: e.tensor_add(
                                scs[:, kt * 512:(kt + 1) * 512],
                                ps_s[:, kt * 512:(kt + 1) * 512],
                                caus[:, 384 - delta: 896 - delta]),
                                waits=[tmm[kt], wT] + sc_free[st], inc=("dve", 1))
                        else:
                            tev = p.emit("act", lambda e, kt=kt, scs=scs: e.copy(
                                scs[:, kt * 512:(kt + 1) * 512],
                                ps_s[:, kt * 512:(kt + 1) * 512]),
                                waits=[tmm[kt]] + sc_free[st], inc=("act", 1))
                        ps_free[kt] = tev
                        evs.append(tev)
                    return (h, qj, st, evs)

                def do_max(ctx):
                    h, qj, st, evs = ctx
                    st2 = (h * 4 + qj) % 3
                    nmx = nmax[st2]
                    tmax = p.emit("dve", lambda e, W=W, nmx=nmx, scs=sc[st]: e.reduce_max(
                        nmx[:], scs[:, 0:W], axis=mybir.AxisListType.X,
                        negate=True), waits=evs + [nmax_free[st2]], inc=("dve", 1))
                    return ctx + (tmax,)

                def do_exp(ctx):
                    h, qj, st, evs, tmax = ctx
                    st2 = (h * 4 + qj) % 3
                    texp = p.emit("act", lambda e, W=W, Ax=Ab2[st2], scs=sc[st],
                                  nmx=nmax[st2], rsm=rsum[st2]: e.activation(
                        Ax[:, 0:W], scs[:, 0:W], AF.Exp, bias=nmx[:],
                        accum_out=rsm[:]),
                        waits=[tmax, abf_free2[st2]], inc=("act", 1))
                    sc_free[st] = [texp]
                    nmax_free[st2] = texp
                    return ctx + (texp,)

                def do_scale(ctx):
                    h, qj, st, evs, tmax, texp = ctx
                    st2 = (h * 4 + qj) % 3
                    trc = p.emit("dve", lambda e, rsm=rsum[st2]: e.reciprocal(rsm[:], rsm[:]),
                                 waits=[texp], inc=("dve", 1))
                    tsc = p.emit("dve", lambda e, W=W, Ax=Ab2[st2], rsm=rsum[st2]: e.tensor_scalar_mul(
                        Ax[:, 0:W], Ax[:, 0:W], rsm[:]),
                        waits=[trc], inc=("dve", 1))
                    return ctx + (tsc,)

                def do_trans(ctx):
                    nonlocal ps_t_free, at_free, abf_free2
                    h, qj, st, evs, tmax, texp, tsc = ctx
                    st2 = (h * 4 + qj) % 3
                    Ax = Ab2[st2]
                    ttr = None
                    for m4 in range(kts):
                        for j in range(4):
                            kb = m4 * 4 + j
                            ttr = p.emit("pe", lambda e, kb=kb, j=j, Ax=Ax: e.transpose(
                                ps_t[:, j * 128:(j + 1) * 128],
                                Ax[:, kb * 128:(kb + 1) * 128], ident[:]),
                                waits=([tsc, ps_t_free] if j == 0 else ()),
                                inc=("pe", 1) if j == 3 else None)
                        tev = p.emit("act", lambda e, m4=m4, qj=qj: e.copy(
                            AT[:].rearrange("p (kb f) -> p kb f", f=512)[
                                :, m4 * 4:(m4 + 1) * 4, qj * 128:(qj + 1) * 128],
                            ps_t[:].rearrange("p (j f) -> p j f", f=128)),
                            waits=[ttr] + at_free, inc=("act", 1))
                        ps_t_free = tev
                    abf_free2[st2] = ttr

                def do_pv(h):
                    nonlocal at_free, ps_pv_free, o_free
                    g, hh = h // 2, h % 2
                    tat = p.tok("act")
                    tpv = None
                    for kb in range(4 * kts):
                        tpv = p.emit("pe", lambda e, kb=kb, h=h, hh=hh, kts=kts: e.matmul(
                            ps_pv[hh * 64:(hh + 1) * 64, :],
                            Vs[:, kb * 1024 + h * 64: kb * 1024 + (h + 1) * 64],
                            AT[:, kb * 512:(kb + 1) * 512],
                            start=(kb == 0), stop=(kb == 4 * kts - 1)),
                            waits=([tat, ps_pv_free] if kb == 0 else ()),
                            inc=("pe", 1) if kb == 4 * kts - 1 else None)
                    at_free = [tpv]
                    if hh == 1:
                        tpe = p.emit("dve", lambda e, g=g: e.tensor_copy(
                            obf[:, g * FT:(g + 1) * FT], ps_pv[:]),
                            waits=[tpv, o_free], inc=("dve", 1))
                        ps_pv_free = tpe

                stages = [do_max, do_exp, do_scale, do_trans]
                pend = []
                for idx in range(64):
                    h, qj = divmod(idx, 4)
                    newp = []
                    done_trans = None
                    for si, ctx in pend:
                        r = stages[si](ctx)
                        if si + 1 < len(stages):
                            newp.append((si + 1, r))
                        else:
                            done_trans = ctx
                    pend = newp
                    if done_trans is not None and done_trans[1] == 3:
                        do_pv(done_trans[0])
                    pend.append((0, do_scores(h, qj)))
                while pend:
                    newp = []
                    done_trans = None
                    for si, ctx in pend:
                        r = stages[si](ctx)
                        if si + 1 < len(stages):
                            newp.append((si + 1, r))
                        else:
                            done_trans = ctx
                    pend = newp
                    if done_trans is not None and done_trans[1] == 3:
                        do_pv(done_trans[0])
                to_ = p.tok("dve")
                ev = {}
                for d in range(8):
                    sl = d % 4
                    tmm = None
                    for kc in range(8):
                        tmm = p.emit("pe", lambda e, d=d, sl=sl, kc=kc: e.matmul(
                            ps_s[:, sl * 512:(sl + 1) * 512],
                            w_o[:, kc * 1024 + d * 128: kc * 1024 + (d + 1) * 128],
                            obf[:, kc * FT:(kc + 1) * FT],
                            start=(kc == 0), stop=(kc == 7)),
                            waits=(([to_, ev.get(sl), y2_free, ps_free.get(sl)])
                                   if kc == 0 else ()),
                            inc=("pe", 1) if kc == 7 else None)
                    ev[sl] = p.emit("dve", lambda e, d=d, sl=sl, par=par: e.tensor_add(
                        y2[:, d * FT:(d + 1) * FT], ps_s[:, sl * 512:(sl + 1) * 512],
                        y1b[par][:, d * FT:(d + 1) * FT]),
                        waits=[tmm, tly, min_store], inc=("dve", 1))
                for sl in range(4):
                    ps_free[sl] = ev[sl]
                o_free = p.tok("pe")
                ty2 = p.tok("dve")
                tr = None
                for q4 in range(4):
                    tr = p.emit("dve", lambda e, q4=q4: e.tensor_add(
                        rbuf[:, q4 * 1024:(q4 + 1) * 1024],
                        y2[:, q4 * 1024:(q4 + 1) * 1024],
                        rbuf[:, q4 * 1024:(q4 + 1) * 1024]),
                        waits=[r_store, ty2], inc=("dve", 1))
                r_store = p.dma("sp", self.dfm(self.r_fm[s][:, :, qt * FT:(qt + 1) * FT]),
                                rbuf[:].rearrange("p (c f) -> p c f", f=FT),
                                "s_Tr%d" % par, waits=[tr])
                if "r" in self.dbg:
                    r_store = p.dma("sp", self.dfm(self.dbg["r"][s][:, :, qt * FT:(qt + 1) * FT]),
                                    rbuf[:].rearrange("p (c f) -> p c f", f=FT),
                                    "s_dbgr", waits=[tr])
                self.sq_free = at_free[0] if at_free else None
                tmin = self.ln_fm(y2, minb, self.ln2c, 0, FT, sqb, rows3, mub, rsb,
                                  scr, ty2)
                y2_free = tmin
                sc_free[0] = [tmin, p.tok("act")]
                sc_free[1] = sc_free[0]
                min_store = p.dma("sp", self.dfm(self.min_d[s][:, :, qt * FT:(qt + 1) * FT]),
                                  minb[:].rearrange("p (c f) -> p c f", f=FT),
                                  "s_Tm%d" % par, waits=[tmin])
                at_free = [tmin, min_store]
                if "y2" in self.dbg:
                    y2_free = p.dma("sp", self.dfm(self.dbg["y2"][s][:, :, qt * FT:(qt + 1) * FT]),
                                    y2[:].rearrange("p (c f) -> p c f", f=FT),
                                    "s_dbgy2", waits=[ty2])
                qfree[0] = p.tok("pe")
            kv_free = p.tok("pe")

    def stage_C(self):
        cfg, p, dr = self.cfg, self.p, self.dr
        S, T = cfg.S, cfg.T
        FT = 512
        HALF = 512
        NH_ = (S * T) // HALF
        NTL = HALF // FT
        sb = self.sb

        self.ps_row = [self.psum("C_row", (1, 512))]
        self.ps_bc = [self.psum("C_bc", (128, 512))]
        ps_bc2 = self.psum("C_bc2", (128, 512))
        self.psrow_free = [None]
        self.psbc_free = [None]
        self.sq_free = None
        ps_g = [self.psum("C_g%d" % i, (128, 512)) for i in range(2)]
        ps_d = [self.psum("C_d%d" % i, (128, 512)) for i in range(2)]
        ps_r = self.psum("C_r", (128, 128))

        bg = sb("C_bg", (128, 8 * 2048), BF16)
        bu = sb("C_bu", (128, 8 * 2048), BF16)
        bd = [sb("C_bd%d" % i, (128, 16 * 512), BF16) for i in range(2)]
        wrt = sb("C_wrt", (128, 8 * 7), BF16)
        minb = [sb("C_min0", (128, 8 * HALF), BF16), self.fxev8]
        acc = sb("C_acc", (128, 8 * HALF))
        tpb = sb("C_tp", (128, 16 * FT), BF16)
        sgs = sb("C_sgs", (128, FT))
        rbt = sb("C_rbt", (128, 8 * FT))
        rows3 = sb("C_rows3", (1, 3 * FT))
        probs = sb("C_probs", (128, 16))
        m1 = sb("C_m1", (128, 1))
        m2 = sb("C_m2", (128, 1))
        msk = sb("C_msk", (128, 16))
        wgt = sb("C_wgt", (128, 8))
        wgtba = sb("C_wgtba", (128, 7 * HALF))
        wgtT = sb("C_wgtT", (8, HALF))
        sel8 = sb("C_sel8", (8, 8 * 128))
        id32 = sb("C_id32", (128, 128))
        rbias_b = sb("C_rbb", (128, 8))
        mub = sb("C_mu", (128, FT))
        rsb = sb("C_rsb", (128, FT))
        scrC = sb("C_scr", (128, FT))
        rb32 = sb("C_rb32", (1, 8))
        p.dma("sp", rb32[0:1, 0:7], dr["p_rbias"][None, :], "s_Cw")
        p.dma("sp", sel8[:], dr["c_sel8"], "s_Cw")
        p.dma("sp", id32[:], dr["c_id32"], "s_Cw")
        wrt32 = sb("C_wrt32", (128, 8 * 7))
        p.dma("sp", wrt32[:].rearrange("p (k l) -> p k l", l=7),
              dr["w_router"].rearrange("k p l -> p k l"), "s_Cw")
        wC = p.tok("s_Cw")
        tms = p.emit("dve", lambda e: e.memset(rb32[0:1, 7:8], 0.0), waits=[wC],
                     inc=("dve", 1))
        tcw = p.emit("dve", lambda e: e.tensor_copy(wrt[:], wrt32[:]),
                     inc=("dve", 1))
        tbb = p.emit("pe", lambda e: e.matmul(ps_r[:, 0:8], self.ones_f[0:1, :],
                                              rb32[0:1, 0:8], start=True, stop=True),
                     waits=[wC, self.ones_tok, tms], inc=("pe", 1))
        tbc = p.emit("dve", lambda e: e.tensor_copy(rbias_b[:], ps_r[:, 0:8]),
                     waits=[tbb], inc=("dve", 1))

        gfree = [None]
        ufree = [None]
        dfree = [None, None]
        accfree = None
        outst = [None]
        psg_free = [None, None]
        psd_free = [None, None]
        tpb_free = None
        sgs_free = None
        mb_last = [None, None]
        psr_free = [tbc]
        psbc2_free = [None]
        ttrq = [None]
        bcast_last = [None]

        def load_min(hf):
            par = hf % 2
            s = (hf * HALF) // T
            t0 = (hf * HALF) % T
            return p.dma("sp", minb[par][:].rearrange("p (c f) -> p c f", f=HALF),
                         self.dfm(self.min_d[s][:, :, t0:t0 + HALF]), "s_Cm",
                         waits=[mb_last[par]])

        def emit_router(hf, tmin):
            mb = minb[hf % 2]
            ttrs = []

            def topk_chain(tb, tlg, po):
                tsg = p.emit("act", lambda e, po=po: e.activation(
                    probs[:, po:po + 7], probs[:, po:po + 7], AF.Sigmoid),
                    waits=[tlg], inc=("act", 1))
                p.emit("dve", lambda e, po=po: e.reduce_max(
                    m1[:], probs[:, po:po + 7], axis=mybir.AxisListType.X),
                    waits=[tsg], inc=("dve", 1))
                p.emit("dve", lambda e, po=po: e.tensor_single_scalar(
                    msk[:, 0:7], probs[:, po:po + 7], m1[:], ALU.is_equal),
                    inc=("dve", 1))
                p.emit("dve", lambda e, po=po: e.scalar_tensor_tensor(
                    msk[:, 8:15], msk[:, 0:7], -1e9, probs[:, po:po + 7],
                    ALU.mult, ALU.add), inc=("dve", 1))
                p.emit("dve", lambda e: e.reduce_max(m2[:], msk[:, 8:15],
                                                     axis=mybir.AxisListType.X),
                       inc=("dve", 1))
                p.emit("dve", lambda e: e.tensor_single_scalar(
                    msk[:, 8:15], msk[:, 8:15], m2[:], ALU.is_equal), inc=("dve", 1))
                p.emit("dve", lambda e: e.tensor_add(wgt[:, 7:8], m1[:], m2[:]),
                       waits=[ttrq[0]], inc=("dve", 1))
                p.emit("dve", lambda e: e.reciprocal(wgt[:, 7:8], wgt[:, 7:8]),
                       inc=("dve", 1))
                p.emit("dve", lambda e: e.tensor_mul(m1[:], m1[:], wgt[:, 7:8]),
                       inc=("dve", 1))
                p.emit("dve", lambda e: e.tensor_mul(m2[:], m2[:], wgt[:, 7:8]),
                       inc=("dve", 1))
                p.emit("dve", lambda e: e.tensor_scalar_mul(
                    wgt[:, 0:7], msk[:, 0:7], m1[:]), inc=("dve", 1))
                tb_ = p.emit("dve", lambda e: e.scalar_tensor_tensor(
                    wgt[:, 0:7], msk[:, 8:15], m2[:], wgt[:, 0:7],
                    ALU.mult, ALU.add), inc=("dve", 1))
                ttr = p.emit("pe", lambda e, tb=tb: e.transpose(
                    self.ps_bc[0][0:8, tb * 128:(tb + 1) * 128], wgt[:, 0:8],
                    id32[:]),
                    waits=[tb_, wC] + ([self.psbc_free[0]] if tb == 0 else []),
                    inc=("pe", 1))
                ttrq[0] = ttr
                ttrs.append(ttr)

            rpend = None
            for tb in range(HALF // 128):
                po = (tb % 2) * 8
                tmm = None
                for kc in range(8):
                    tmm = p.emit("pe", lambda e, tb=tb, kc=kc, mb=mb: e.matmul(
                        ps_r[:, 0:7],
                        mb[:, kc * HALF + tb * 128: kc * HALF + tb * 128 + 128],
                        wrt[:, kc * 7:(kc + 1) * 7],
                        start=(kc == 0), stop=(kc == 7)),
                        waits=([tmin, tcw, psr_free[0]] if kc == 0 else ()),
                        inc=("pe", 1) if kc == 7 else None)
                tlg = p.emit("dve", lambda e, po=po: e.tensor_add(
                    probs[:, po:po + 7], ps_r[:, 0:7], rbias_b[:, 0:7]),
                    waits=[tmm, tbc], inc=("dve", 1))
                psr_free[0] = tlg
                if rpend is not None:
                    topk_chain(*rpend)
                rpend = (tb, tlg, po)
            topk_chain(*rpend)
            tcp = p.emit("dve", lambda e: e.tensor_copy(wgtT[:],
                                                        self.ps_bc[0][0:8, :]),
                         waits=[ttrs[-1], bcast_last[0]], inc=("dve", 1))
            twb = None
            bankfree = [tcp, psbc2_free[0]]
            for e_ in range(7):
                bank = self.ps_bc[0] if e_ % 2 == 0 else ps_bc2
                tbc2 = p.emit("pe", lambda e, e_=e_, bank=bank: e.matmul(
                    bank[:, 0:HALF], sel8[:, e_ * 128:(e_ + 1) * 128], wgtT[:],
                    start=True, stop=True),
                    waits=[tcp, wC, bankfree[e_ % 2]], inc=("pe", 1))
                twb = p.emit("dve", lambda e, e_=e_, bank=bank: e.tensor_copy(
                    wgtba[:, e_ * HALF:(e_ + 1) * HALF], bank[:, 0:HALF]),
                    waits=[tbc2], inc=("dve", 1))
                bankfree[e_ % 2] = twb
                if e_ % 2 == 0:
                    self.psbc_free[0] = twb
                else:
                    psbc2_free[0] = twb
                bcast_last[0] = tbc2
            return (tmin, twb)

        tmin0 = load_min(0)
        rstate = [emit_router(0, tmin0)]
        for hf in range(NH_):
            par = hf % 2
            mb = minb[par]
            s = (hf * HALF) // T
            t0 = (hf * HALF) % T
            ctmin, ctwb = rstate[0]
            tmin_next = load_min(hf + 1) if hf + 1 < NH_ else None
            trl = p.dma("sp", rbt[:].rearrange("p (c f) -> p c f", f=FT),
                        self.dfm(self.r_fm[s][:, :, t0:t0 + HALF]),
                        "s_Cr", waits=[outst[0]])
            glast = ulast = None
            for e_ in range(8):
                tg_ = p.dma("sp", bg[:].rearrange("p (k l) -> p k l", l=2048),
                            self.wb["w_moe_g"][e_].rearrange("k p l -> p k l"),
                            "s_Cg", waits=[gfree[0]] + (self.cast_done["w_moe_g"] if hf == 0 and e_ == 0 else []))
                tu_ = p.dma("sp", bu[:].rearrange("p (k l) -> p k l", l=2048),
                            self.wb["w_moe_u"][e_].rearrange("k p l -> p k l"),
                            "s_Cu", waits=[ufree[0]] + (self.cast_done["w_moe_u"] if hf == 0 and e_ == 0 else []))
                td_ = [None, None]
                for dh in range(2):
                    td_[dh] = p.dma(
                        "sp", bd[dh][:].rearrange("p (k l) -> p k l", l=512),
                        self.wb["w_moe_d"][e_].rearrange("k p l -> p k l")[:, :, dh * 512:(dh + 1) * 512],
                        "s_Cd%d" % dh,
                        waits=[dfree[dh]] + (self.cast_done["w_moe_d"] if hf == 0 and e_ == 0 else []))
                for tl in range(NTL):
                    tglast = None
                    for c in range(16):
                        gp = c % 2
                        tgm = tum = None
                        for kc in range(8):
                            tgm = p.emit("pe", lambda e, c=c, kc=kc, tl=tl, gp=gp, mb=mb: e.matmul(
                                ps_g[gp][:, 0:FT],
                                bg[:, kc * 2048 + c * 128: kc * 2048 + (c + 1) * 128],
                                mb[:, kc * HALF + tl * FT: kc * HALF + (tl + 1) * FT],
                                start=(kc == 0), stop=(kc == 7)),
                                waits=([tg_, ctmin, psg_free[gp]] if kc == 0 else ()),
                                inc=("pe", 1) if kc == 7 else None)
                        if tl == NTL - 1 and c == 15:
                            glast = tgm
                        tsl = p.emit("act", lambda e, gp=gp: e.activation(
                            sgs[:], ps_g[gp][:, 0:FT], AF.Silu),
                            waits=[tgm, sgs_free], inc=("act", 1))
                        for kc in range(8):
                            tum = p.emit("pe", lambda e, c=c, kc=kc, tl=tl, gp=gp, mb=mb: e.matmul(
                                ps_d[gp][:, 0:FT],
                                bu[:, kc * 2048 + c * 128: kc * 2048 + (c + 1) * 128],
                                mb[:, kc * HALF + tl * FT: kc * HALF + (tl + 1) * FT],
                                start=(kc == 0), stop=(kc == 7)),
                                waits=([tu_, psd_free[gp]] if kc == 0 else ()),
                                inc=("pe", 1) if kc == 7 else None)
                        if tl == NTL - 1 and c == 15:
                            ulast = tum
                        tt_ = p.emit("dve", lambda e, c=c, gp=gp: e.tensor_mul(
                            tpb[:, c * FT:(c + 1) * FT], sgs[:], ps_d[gp][:, 0:FT]),
                            waits=[tsl, tum, tpb_free], inc=("dve", 1))
                        psg_free[gp] = tt_
                        sgs_free = tt_
                        if e_ < 7:
                            tt_ = p.emit("dve", lambda e, c=c, tl=tl, e_=e_: e.tensor_mul(
                                tpb[:, c * FT:(c + 1) * FT], tpb[:, c * FT:(c + 1) * FT],
                                wgtba[:, e_ * HALF + tl * FT: e_ * HALF + (tl + 1) * FT]),
                                waits=[ctwb], inc=("dve", 1))
                        psd_free[gp] = tt_
                        tglast = tt_
                    for d in range(8):
                        gp = d % 2
                        dh = d // 4
                        tdm = None
                        for c in range(16):
                            tdm = p.emit("pe", lambda e, d=d, c=c, gp=gp, dh=dh: e.matmul(
                                ps_g[gp][:, 0:FT],
                                bd[dh][:, c * 512 + (d % 4) * 128: c * 512 + (d % 4 + 1) * 128],
                                tpb[:, c * FT:(c + 1) * FT],
                                start=(c == 0), stop=(c == 15)),
                                waits=([td_[dh], tglast, psg_free[gp]] if c == 0 else ()),
                                inc=("pe", 1) if c == 15 else None)
                        if e_ == 0:
                            tac = p.emit("dve", lambda e, d=d, tl=tl, gp=gp: e.tensor_copy(
                                acc[:, d * HALF + tl * FT: d * HALF + (tl + 1) * FT],
                                ps_g[gp][:, 0:FT]),
                                waits=[tdm, accfree], inc=("dve", 1))
                        else:
                            tac = p.emit("dve", lambda e, d=d, tl=tl, gp=gp: e.tensor_add(
                                acc[:, d * HALF + tl * FT: d * HALF + (tl + 1) * FT],
                                acc[:, d * HALF + tl * FT: d * HALF + (tl + 1) * FT],
                                ps_g[gp][:, 0:FT]),
                                waits=[tdm], inc=("dve", 1))
                        psg_free[gp] = tac
                tpb_free = p.tok("pe")
                gfree = [glast]
                ufree = [ulast]
                dfree = [p.tok("pe"), p.tok("pe")]
            mb_last[par] = ulast
            if hf + 1 < NH_:
                rstate[0] = emit_router(hf + 1, tmin_next)
            tacc = p.tok("dve")
            for tl in range(NTL):
                tsum = None
                for c in range(8):
                    tsum = p.emit("dve", lambda e, c=c, tl=tl: e.tensor_add(
                        rbt[:, c * FT:(c + 1) * FT], rbt[:, c * FT:(c + 1) * FT],
                        acc[:, c * HALF + tl * FT: c * HALF + (tl + 1) * FT]),
                        waits=([trl, tacc] if c == 0 else ()),
                        inc=("dve", 1))
                sqv = tpb[:, 0:8 * FT]
                self.sq_free = tpb_free
                tln = self.ln_fm(rbt, rbt, None, 0, FT, sqv, rows3, mub, rsb, scrC,
                                 tsum)
                to_ = p.dma("sp", self.dfm(self.out[s][:, :, t0 + tl * FT: t0 + (tl + 1) * FT]),
                            rbt[:].rearrange("p (c f) -> p c f", f=FT), "s_Cout",
                            waits=[tln])
                outst = [to_]
            accfree = p.tok("dve")
            tpb_free = p.tok("pe")

    def build(self):
        self.decl_dram()
        es = ExitStack()
        with es:
            self.es = es
            self.p = Prog(self.nc, es)
            self.psg_free = [None, None]
            self.psrow_free = [None, None]
            self.psbc_free = [None, None]
            self.sq_free = None
            self.guard = []

            self.stage_prep()
            if "A" in self.cfg.stages:
                with ExitStack() as ses:
                    old, self.es = self.es, ses
                    self.stage_A1()
                    self.es = old
                self.stage_scope()
            if "A2" in self.cfg.stages:
                with ExitStack() as ses:
                    old, self.es = self.es, ses
                    self.stage_A2()
                    self.es = old
                self.stage_scope()
            if "B1" in self.cfg.stages:
                with ExitStack() as ses:
                    old, self.es = self.es, ses
                    self.stage_B1()
                    self.es = old
                self.stage_scope()
            if "B2" in self.cfg.stages:
                with ExitStack() as ses:
                    old, self.es = self.es, ses
                    self.stage_B2()
                    self.es = old
                self.stage_scope()
            if "C" in self.cfg.stages:
                with ExitStack() as ses:
                    old, self.es = self.es, ses
                    self.stage_C()
                    self.es = old
                self.stage_scope()
            for s in list(self.p.sems):
                if s.startswith("s_"):
                    self.p.emit("sp", None, waits=[self.p.tok(s)])
            blk = es.enter_context(self.nc.Block())
            self.p.playback(blk)
        return self.nc


def build(cfg):
    return Builder(cfg).build()


def kernel(**inputs):
    cfg = Cfg()
    nc = build(cfg)
    in_maps = [host_prepare(inputs, core, cfg.S, cfg.T) for core in range(NCORE)]
    res = run_bass_kernel_spmd(nc, in_maps, core_ids=list(range(NCORE)))
    outs = []
    for core in range(NCORE):
        o = np.asarray(res.results[core]["out"])
        o = o.reshape(cfg.S, 1024, cfg.T).transpose(0, 2, 1)
        outs.append(o)
    return np.concatenate(outs, 0).astype(np.float32)


# version 17 (final)
# speedup vs baseline: 1.1773x; 1.1773x over previous
import numpy as np
from contextlib import ExitStack

import concourse.bass as bass
import concourse.mybir as mybir
from concourse.bass_utils import run_bass_kernel_spmd

F32 = mybir.dt.float32
BF16 = mybir.dt.bfloat16
AF = mybir.ActivationFunctionType
ALU = mybir.AluOpType

NCORE = 8
ENGINES = ("sp", "pe", "dve", "act", "pool")


class Prog:
    def __init__(self, nc, es):
        self.nc = nc
        self.es = es
        self.ops = {e: [] for e in ENGINES}
        self.sems = {}
        self.cnt = {}
        self.hwm = {e: {} for e in ENGINES}

    def sem(self, name):
        if name not in self.sems:
            self.sems[name] = self.es.enter_context(self.nc.semaphore(name))
            self.cnt[name] = 0
        return name

    SERIAL = ("dve", "act", "pool")

    def emit(self, eng, fn, waits=(), inc=None, is_dma=False):
        waits = list(waits)
        if fn is not None and eng in self.SERIAL and not is_dma:
            self.sem(eng)
            waits.append((eng, self.cnt[eng]))
            inc = (eng, 1)
        real = []
        for t in waits:
            if t is None:
                continue
            s, v = t
            if v <= 0 or self.hwm[eng].get(s, 0) >= v:
                continue
            self.hwm[eng][s] = v
            real.append((s, v))
        tok = None
        if inc is not None:
            s, d = inc
            self.sem(s)
            self.cnt[s] += d
            tok = (s, self.cnt[s])
        self.ops[eng].append((real, fn, inc))
        return tok

    def dma(self, eng, dst, src, slot, waits=(), **kw):
        self.sem(slot)
        throttle = (slot, self.cnt[slot])
        fn = lambda e, d=dst, s=src, kw_=dict(kw): e.dma_start(d, s, **kw_)
        return self.emit(eng, fn, waits=list(waits) + [throttle], inc=(slot, 16),
                         is_dma=True)

    def tok(self, sem):
        return (sem, self.cnt.get(sem, 0))

    def playback(self, block):
        def mk(name):
            def body(eng):
                for waits, fn, inc in self.ops[name]:
                    for s, v in waits:
                        eng.wait_ge(self.sems[s], v)
                    if fn is None:
                        continue
                    ins = fn(eng)
                    if inc is not None:
                        ins.then_inc(self.sems[inc[0]], inc[1])
            return body
        block.sync(mk("sp"))
        block.tensor(mk("pe"))
        block.vector(mk("dve"))
        block.scalar(mk("act"))
        block.gpsimd(mk("pool"))


def _lhsT(w):
    dout, din = w.shape
    return np.ascontiguousarray(w.T.reshape(din // 128, 128, dout))


def host_prepare(inputs, core, S=4, T=2048):
    import ml_dtypes
    I = inputs
    x = np.asarray(I["L0_attn_x"], np.float32)
    xs = x[core * S:(core + 1) * S]
    x_fm = np.ascontiguousarray(xs.transpose(0, 2, 1)).reshape(S, 8, 128, T)

    rope_q = np.asarray(I["L0_ffn_rope_q_w"], np.float32)
    rope_k = np.asarray(I["L0_ffn_rope_k_w"], np.float32)
    q_u = np.asarray(I["L0_ffn_q_u"], np.float32)
    k_u = np.asarray(I["L0_ffn_k_u"], np.float32)

    w1q, w2q, w1k, w2k = [], [], [], []
    for g in range(8):
        for h in (2 * g, 2 * g + 1):
            w1q += [rope_q[h * 32: h * 32 + 32], q_u[h * 64 + 32: h * 64 + 64]]
            w2q += [rope_q[h * 32 + 16: h * 32 + 32], rope_q[h * 32: h * 32 + 16],
                    np.zeros((32, 256), np.float32)]
            w1k += [rope_k[h * 32: h * 32 + 32], np.zeros((32, 1024), np.float32)]
            w2k += [rope_k[h * 32 + 16: h * 32 + 32], rope_k[h * 32: h * 32 + 16],
                    np.zeros((32, 1024), np.float32)]
    w1q = np.concatenate(w1q, 0); w2q = np.concatenate(w2q, 0)
    w1k = np.concatenate(w1k, 0); w2k = np.concatenate(w2k, 0)
    k_u_sel = np.concatenate([k_u[h * 64 + 32: h * 64 + 64] for h in range(16)], 0)

    gate_all = np.concatenate([np.asarray(I["L0_ffn_routed_gate"], np.float32),
                               np.asarray(I["L0_ffn_shared_gate"], np.float32)], 0)
    up_all = np.concatenate([np.asarray(I["L0_ffn_routed_up"], np.float32),
                             np.asarray(I["L0_ffn_shared_up"], np.float32)], 0)
    down_all = np.concatenate([np.asarray(I["L0_ffn_routed_down"], np.float32),
                               np.asarray(I["L0_ffn_shared_down"], np.float32)], 0)

    inv = (1.0 / (np.float32(10000.0) **
                  (np.arange(0, 32, 2, dtype=np.float32) / np.float32(32)))).astype(np.float32)
    t = np.arange(T, dtype=np.float32)
    fr = (t[None, :] * inv[:, None]).astype(np.float32)
    cos16 = np.cos(fr).astype(np.float32)
    sin16 = np.sin(fr).astype(np.float32)
    cosM = np.empty((128, T), np.float32)
    sinM = np.empty((128, T), np.float32)
    for blk in (0, 64):
        cosM[blk:blk + 16] = cos16; cosM[blk + 16:blk + 32] = cos16
        cosM[blk + 32:blk + 64] = 1.0
        sinM[blk:blk + 16] = sin16; sinM[blk + 16:blk + 32] = sin16
        sinM[blk + 32:blk + 64] = 0.0
    u = np.arange(896)[None, :] - 384
    pp = np.arange(128)[:, None]
    caus = np.where(u <= pp, np.float32(0.0), np.float32(-1e30)).astype(np.float32)

    m = {
        "x": x_fm,
        "w_in": _lhsT(np.asarray(I["L0_attn_input_w"], np.float32)),
        "w_gates": _lhsT(np.asarray(I["L0_attn_gates_w"], np.float32)),
        "w_out": _lhsT(np.asarray(I["L0_attn_output_w"], np.float32)),
        "w_grow": _lhsT(np.asarray(I["L0_attn_grow_w"], np.float32)),
        "w_shrink": _lhsT(np.asarray(I["L0_attn_shrink_w"], np.float32)),
        "w_kvd": _lhsT(np.asarray(I["L0_ffn_kv_d"], np.float32)),
        "w_qd": _lhsT(np.asarray(I["L0_ffn_q_d"], np.float32)),
        "w_vu": np.ascontiguousarray(np.asarray(I["L0_ffn_v_u"], np.float32).T.reshape(2, 128, 1024)),
        "w_q1": _lhsT(w1q), "w_q2": _lhsT(w2q),
        "w_k1": _lhsT(w1k), "w_k2": _lhsT(w2k),
        "w_kusel": np.ascontiguousarray(k_u_sel.T.reshape(2, 128, 512)),
        "w_kvd_l": np.ascontiguousarray(np.asarray(I["L0_ffn_kv_d"], np.float32).reshape(2, 128, 1024)),
        "w_o": _lhsT(np.asarray(I["L0_ffn_o_w"], np.float32)),
        "w_router": np.ascontiguousarray(np.asarray(I["L0_ffn_router_w"], np.float32).T.reshape(8, 128, 7)),
        "w_moe_g": np.stack([_lhsT(gate_all[e]) for e in range(8)]),
        "w_moe_u": np.stack([_lhsT(up_all[e]) for e in range(8)]),
        "w_moe_d": np.stack([_lhsT(down_all[e]) for e in range(8)]),
        "p_hawk_g": np.asarray(I["L0_attn_hawk_norm_gamma"], np.float32),
        "p_gmlp_g": np.asarray(I["L0_attn_gmlp_norm_gamma"], np.float32),
        "p_conv_w": np.ascontiguousarray(np.asarray(I["L0_attn_conv_w"], np.float32).reshape(1024, 4)),
        "p_conv_b": np.asarray(I["L0_attn_conv_b"], np.float32),
        "p_gates_b": np.asarray(I["L0_attn_gates_b"], np.float32),
        "p_fb": np.asarray(I["L0_attn_forget_base"], np.float32),
        "p_ln1": np.asarray(I["L0_ffn_ln1_w"], np.float32),
        "p_ln2": np.asarray(I["L0_ffn_ln2_w"], np.float32),
        "p_rbias": np.asarray(I["L0_ffn_routing_bias"], np.float32),
        "c_ident": np.eye(128).astype(ml_dtypes.bfloat16),
        "c_id32": np.eye(128).astype(np.float32),
        "c_sel8": _sel8(),
        "c_cos": cosM, "c_sin": sinM, "c_caus": caus,
    }
    return {k: np.ascontiguousarray(v) for k, v in m.items()}


def _sel8():
    s = np.zeros((8, 8 * 128), np.float32)
    for e in range(8):
        s[e, e * 128:(e + 1) * 128] = 1.0
    return s


class Cfg:
    def __init__(self, S=4, T=2048, stages=("A", "A2", "B1", "B2", "C"), dbg=()):
        self.S, self.T = S, T
        self.stages, self.dbg = tuple(stages), tuple(dbg)


class Builder:
    def __init__(self, cfg):
        self.cfg = cfg
        self.nc = bass.Bass("TRN2", target_bir_lowering=False)
        self.dr = {}
        self.wb = {}
        self.dbg = {}
        self.cast_done = {}

    def decl_dram(self):
        nc, cfg = self.nc, self.cfg
        S, T = cfg.S, cfg.T

        def din(name, shape, dt=F32):
            self.dr[name] = nc.dram_tensor(name, list(shape), dt, kind="ExternalInput").ap()
        din("x", (S, 8, 128, T))
        din("w_in", (8, 128, 2048)); din("w_gates", (8, 128, 2048)); din("w_out", (8, 128, 1024))
        din("w_grow", (8, 128, 4096)); din("w_shrink", (16, 128, 1024))
        din("w_kvd", (8, 128, 256)); din("w_qd", (8, 128, 256)); din("w_vu", (2, 128, 1024))
        din("w_q1", (2, 128, 1024)); din("w_q2", (2, 128, 1024))
        din("w_k1", (8, 128, 1024)); din("w_k2", (8, 128, 1024))
        din("w_kusel", (2, 128, 512)); din("w_kvd_l", (2, 128, 1024))
        din("w_o", (8, 128, 1024)); din("w_router", (8, 128, 7))
        din("w_moe_g", (8, 8, 128, 2048)); din("w_moe_u", (8, 8, 128, 2048))
        din("w_moe_d", (8, 16, 128, 1024))
        for nm in ("p_hawk_g", "p_gmlp_g", "p_conv_b", "p_fb", "p_ln1", "p_ln2"):
            din(nm, (1024,))
        din("p_conv_w", (1024, 4)); din("p_gates_b", (2048,)); din("p_rbias", (7,))
        din("c_ident", (128, 128), BF16)
        din("c_id32", (128, 128)); din("c_sel8", (8, 8 * 128))
        din("c_cos", (128, T)); din("c_sin", (128, T)); din("c_caus", (128, 896))

        self.out = nc.dram_tensor("out", [S, 8, 128, T], F32, kind="ExternalOutput").ap()
        for name in cfg.dbg:
            self.dbg[name] = nc.dram_tensor("dbg_" + name, [S, 8, 128, T], F32,
                                            kind="ExternalOutput").ap()

        def scr(name, shape, dt):
            return nc.dram_tensor(name, list(shape), dt, kind="Internal").ap()
        self.x1_fm = scr("x1_fm", (S, 8, 128, T), F32)
        self.y1_fm = scr("y1_fm", (S, 8, 128, T), F32)
        self.ain_d = scr("ain_d", (S, 8, 128, T), BF16)
        self.q_d = scr("q_dd", (S, 8, 128, T), BF16)
        self.k_dd = scr("k_dd", (S, 8, 128, T), BF16)
        self.v_dd = scr("v_dd", (S, T // 128, 128, 1024), BF16)
        self.min_d = scr("min_dd", (S, 8, 128, T), BF16)
        self.r_fm = scr("r_fm", (S, 8, 128, T), F32)
        self.wgt_d = scr("wgt_d", (max(1, S * T // 512), 4, 128, 8), F32)
        for nm, shape in [("w_in", (8, 128, 2048)), ("w_gates", (8, 128, 2048)),
                          ("w_out", (8, 128, 1024)), ("w_grow", (8, 128, 4096)),
                          ("w_shrink", (16, 128, 1024)), ("w_kvd", (8, 128, 256)),
                          ("w_qd", (8, 128, 256)), ("w_vu", (2, 128, 1024)),
                          ("w_q1", (2, 128, 1024)), ("w_q2", (2, 128, 1024)),
                          ("w_k1", (8, 128, 1024)), ("w_k2", (8, 128, 1024)),
                          ("w_o", (8, 128, 1024)),
                          ("w_moe_g", (8, 8, 128, 2048)), ("w_moe_u", (8, 8, 128, 2048)),
                          ("w_moe_d", (8, 16, 128, 1024))]:
            self.wb[nm] = scr(nm + "_bf", shape, BF16)

    def sb(self, name, shape, dt=F32):
        return self.es.enter_context(self.nc.sbuf_tensor(name, list(shape), dt))

    def psum(self, name, shape, dt=F32):
        return self.es.enter_context(self.nc.psum_tensor(name, list(shape), dt))

    @staticmethod
    def dfm(ap):
        return ap.rearrange("c p f -> p c f")

    @staticmethod
    def flatP(ap):
        nd = len(ap.shape)
        if nd == 2:
            return ap[:, None, :]
        if nd == 3:
            return ap.rearrange("a p l -> p a l")
        if nd == 4:
            return ap.rearrange("a b p l -> p (a b) l")
        raise ValueError(ap.shape)

    def tap(self, name, buf, tok, s, tt, FT):
        if name not in self.dbg:
            return
        t = self.p.dma("sp", self.dfm(self.dbg[name][s][:, :, tt * FT:(tt + 1) * FT]),
                       buf[:].rearrange("p (c f) -> p c f", f=FT), "s_tap_" + name,
                       waits=[tok])
        self.guard.append(t)
        return t

    def mk_ps(self, tag, row=True, bc=True, g=True):
        if row:
            self.ps_row = [self.psum(f"{tag}_row{i}", (1, 512)) for i in range(2)]
        if bc:
            self.ps_bc = [self.psum(f"{tag}_bc{i}", (128, 512)) for i in range(2)]
        if g:
            self.ps_g = [self.psum(f"{tag}_g{i}", (128, 512)) for i in range(2)]
        self.psg_free = [None, None]
        self.psrow_free = [None, None]
        self.psbc_free = [None, None]
        self.sq_free = None

    def stage_scope(self, slot_sems=None):
        p = self.p
        slots = [s for s in p.sems if s.startswith("s_")]
        toks = [p.tok("pe"), p.tok("dve"), p.tok("act")] + [p.tok(s) for s in slots]
        for e in ("sp", "pe", "dve", "act"):
            for t in toks:
                p.emit(e, None, waits=[t])
        return toks

    def stage_prep(self):
        p, dr = self.p, self.dr
        sb = self.sb

        self.ones_b = sb("ones_b", (128, 1), BF16)
        self.ones_f = sb("ones_f", (1, 128), F32)
        self.gh = sb("gh", (128, 8))
        self.gg2 = sb("gg2", (128, 8))
        self.convw = sb("convw", (128, 32))
        self.convb = sb("convb", (128, 8))
        self.gatesb = sb("gatesb", (128, 16))
        self.acoef = sb("acoef", (128, 8))
        self.ln1c = sb("ln1c", (128, 8))
        self.ln2c = sb("ln2c", (128, 8))
        self.rbias = sb("rbias", (1, 7))
        self.fxev8 = sb("fxev8", (128, 8 * 512), BF16)

        def col_load(dst, name):
            src = dr[name].rearrange("(c p) -> p c", p=128)
            return p.dma("pool", dst[:, 0:src.shape[1]], src, "s_small",
                         allow_slow_non_contiguous=True)
        col_load(self.gh, "p_hawk_g")
        col_load(self.gg2, "p_gmlp_g")
        col_load(self.convb, "p_conv_b")
        col_load(self.acoef, "p_fb")
        col_load(self.ln1c, "p_ln1")
        col_load(self.ln2c, "p_ln2")
        p.dma("pool", self.convw[:].rearrange("p (c k) -> p c k", k=4),
              dr["p_conv_w"].rearrange("(c p) k -> p c k", p=128),
              "s_small", allow_slow_non_contiguous=True)
        p.dma("pool", self.gatesb[:], dr["p_gates_b"].rearrange("(c p) -> p c", p=128),
              "s_small", allow_slow_non_contiguous=True)
        p.dma("pool", self.rbias[:], dr["p_rbias"][None, :], "s_small")
        self.small_tok = p.tok("s_small")

        p.emit("dve", lambda e: e.memset(self.ones_b[:], 1.0))
        self.c1eps = self.sb("c1eps", (128, 1))
        p.emit("dve", lambda e: e.memset(self.c1eps[:], 1.0 + 1e-6))
        p.emit("dve", lambda e: e.memset(self.ones_f[:], 1.0), inc=("dve", 1))
        self.ones_tok = p.tok("dve")
        p.emit("act", lambda e: e.activation(self.acoef[:], self.acoef[:], AF.Exp),
               waits=[self.small_tok])
        tln = p.emit("act", lambda e: e.activation(self.acoef[:], self.acoef[:], AF.Ln,
                                                   bias=1.0), inc=("act", 1))
        p.emit("dve", lambda e: e.tensor_scalar_mul(self.acoef[:], self.acoef[:], -8.0),
               waits=[tln], inc=("dve", 1))
        self.acoef_tok = p.tok("dve")

        with ExitStack() as fes:
            old_es, self.es = self.es, fes
            kvdl32 = self.sb("kvdl32", (128, 2048))
            kusl32 = self.sb("kusl32", (128, 1024))
            kvdl = self.sb("kvdl", (128, 2048), BF16)
            kusl = self.sb("kusl", (128, 1024), BF16)
            psf = self.psum("psf", (128, 512))
            t1 = p.dma("pool", kvdl32[:].rearrange("p (a l) -> p a l", l=1024),
                       self.flatP(dr["w_kvd_l"]), "s_fx2")
            t2 = p.dma("pool", kusl32[:].rearrange("p (a l) -> p a l", l=512),
                       self.flatP(dr["w_kusel"]), "s_fx2")
            p.emit("pool", lambda e: e.tensor_copy(kvdl[:], kvdl32[:]), waits=[t1, t2])
            c2 = p.emit("pool", lambda e: e.tensor_copy(kusl[:], kusl32[:]), inc=("pool", 1))
            te = None
            for dchunk in range(8):
                tk = None
                for lc in range(2):
                    tk = p.emit("pe", lambda e, d=dchunk, lc=lc: e.matmul(
                        psf[:], kvdl[:, lc * 1024 + d * 128: lc * 1024 + (d + 1) * 128],
                        kusl[:, lc * 512:(lc + 1) * 512], start=(lc == 0), stop=(lc == 1)),
                        waits=([c2, te] if lc == 0 else ()),
                        inc=("pe", 1) if lc == 1 else None)
                te = p.emit("dve", lambda e, d=dchunk: e.tensor_copy(
                    self.fxev8[:, d * 512:(d + 1) * 512], psf[:]),
                    waits=[tk], inc=("dve", 1))
            self.fusedk_tok = te
            self.es = old_es

        CASTW = 512
        self.wst32 = [self.sb(f"wst32_{i}", (128, CASTW)) for i in range(2)]
        self.wst16 = [self.sb(f"wst16_{i}", (128, CASTW), BF16) for i in range(2)]
        gamma_fold = {"w_in": (self.gh, 32.0), "w_grow": (self.gg2, 32.0),
                      "w_q1": (None, 0.125), "w_q2": (None, 0.125)}
        plan = ["w_in", "w_gates", "w_out", "w_grow", "w_shrink",
                "w_q1", "w_q2", "w_kvd", "w_qd", "w_vu", "w_k1", "w_k2", "w_o",
                "w_moe_g", "w_moe_u", "w_moe_d"]
        for s in ("s_wl0", "s_wl1", "s_ws0", "s_ws1"):
            p.sem(s)
        slabs = []
        for nm in plan:
            src3 = self.flatP(self.dr[nm])
            dst3 = self.flatP(self.wb[nm])
            A, L = src3.shape[1], src3.shape[2]
            nkc = self.dr[nm].shape[0] if len(self.dr[nm].shape) == 3 else self.dr[nm].shape[1]
            fold = gamma_fold.get(nm)
            for a in range(A):
                for off in range(0, L, CASTW):
                    w = min(CASTW, L - off)
                    slabs.append((nm, src3, dst3, a, off, w, fold,
                                  (a % nkc) if fold is not None and fold[0] is not None else 0))
        n_slabs = len(slabs)
        cast_toks = [None] * n_slabs

        def emit_load(i):
            nm, src3, dst3, a, off, w, fold, kc = slabs[i]
            b = i % 2
            p.dma("pool", self.wst32[b][:, 0:w], src3[:, a, off:off + w],
                  f"s_wl{b}", waits=[cast_toks[i - 2] if i >= 2 else None,
                                     self.fusedk_tok])
        emit_load(0)
        if n_slabs > 1:
            emit_load(1)
        last_store = {}
        for i in range(n_slabs):
            nm, src3, dst3, a, off, w, fold, kc = slabs[i]
            b = i % 2
            gen = i // 2
            if fold is None:
                ceng = "pool"
                fn = lambda e, b=b, w=w: e.tensor_copy(self.wst16[b][:, 0:w],
                                                       self.wst32[b][:, 0:w])
            elif fold[0] is None:
                ceng = "dve"
                fn = lambda e, b=b, w=w, sc=fold[1]: e.tensor_scalar_mul(
                    self.wst16[b][:, 0:w], self.wst32[b][:, 0:w], sc)
            else:
                ceng = "dve"
                fn = lambda e, b=b, w=w, kc=kc, fold=fold: e.tensor_scalar(
                    self.wst16[b][:, 0:w], self.wst32[b][:, 0:w],
                    fold[0][:, kc:kc + 1], fold[1], ALU.mult, ALU.mult)
            cast_toks[i] = p.emit(ceng, fn,
                                  waits=[(f"s_wl{b}", 16 * (gen + 1)),
                                         (f"s_ws{b}", 16 * gen), self.small_tok])
            p.dma("pool", dst3[:, a, off:off + w], self.wst16[b][:, 0:w],
                  f"s_ws{b}", waits=[cast_toks[i]])
            if i + 2 < n_slabs:
                emit_load(i + 2)
            last_store[nm] = [p.tok("s_ws0"), p.tok("s_ws1")]
        for nm in plan:
            self.cast_done[nm] = last_store[nm]

        fix = self.sb("fixt", (128, 1024), BF16)
        for nm, kcs in (("w_q2", 2), ("w_k2", 8)):
            for c in range(kcs):
                tl = p.dma("sp", fix[:], self.wb[nm][c], "s_fix",
                           waits=self.cast_done[nm] + [self.fusedk_tok])
                v = fix[:].rearrange("p (g f) -> p g f", f=64)
                tfx = p.emit("dve",
                             lambda e, v=v: e.tensor_scalar_mul(v[:, :, 0:16],
                                                                v[:, :, 0:16], -1.0),
                             waits=[tl], inc=("dve", 1))
                p.dma("sp", self.wb[nm][c], fix[:], "s_fix", waits=[tfx])
        self.fix_done = p.tok("s_fix")

        for dchunk in range(8):
            dst = self.wb["w_k1"][dchunk].rearrange("p (g h x f) -> p g h x f",
                                                    h=2, x=2, f=32)[:, :, :, 1, :]
            src = self.fxev8[:, dchunk * 512:(dchunk + 1) * 512].rearrange(
                "p (g h f) -> p g h f", h=2, f=32)
            p.dma("sp", dst, src, "s_fx3",
                  waits=[self.fusedk_tok] + self.cast_done["w_k1"],
                  allow_slow_non_contiguous=True)
        self.k1_done = p.tok("s_fx3")

    def ln_fm(self, src, dst, gammac, par, FT, sqb, rows, mub, rsb, scr, in_tok,
              out_bf=False):
        p = self.p
        psr, psb = self.ps_row[par], self.ps_bc[par]
        tcv = p.emit("act", lambda e: e.copy(sqb[:], src[:]),
                     waits=[in_tok, self.sq_free], inc=("act", 1))
        tsum = None
        for kc in range(8):
            tsum = p.emit("pe", lambda e, kc=kc: e.matmul(
                psr[0:1, 0:FT], self.ones_b[:], sqb[:, kc * FT:(kc + 1) * FT],
                start=(kc == 0), stop=(kc == 7)),
                waits=([tcv, self.psrow_free[par], self.ones_tok] if kc == 0 else ()),
                inc=("pe", 1) if kc == 7 else None)
        tmu = p.emit("dve", lambda e: e.tensor_scalar_mul(rows[0:1, 0:FT],
                                                          psr[0:1, 0:FT], 1.0 / 1024),
                     waits=[tsum], inc=("dve", 1))
        tsq = p.emit("act", lambda e: e.activation(sqb[:], sqb[:], AF.Square),
                     waits=[tsum], inc=("act", 1))
        tss = None
        for kc in range(8):
            tss = p.emit("pe", lambda e, kc=kc: e.matmul(
                psr[0:1, 0:FT], self.ones_b[:], sqb[:, kc * FT:(kc + 1) * FT],
                start=(kc == 0), stop=(kc == 7)),
                waits=([tsq, tmu] if kc == 0 else ()),
                inc=("pe", 1) if kc == 7 else None)
        self.sq_free = tss
        tmu2 = p.emit("dve", lambda e: e.tensor_mul(rows[0:1, FT:2 * FT],
                                                    rows[0:1, 0:FT], rows[0:1, 0:FT]),
                      inc=("dve", 1))
        tvar = p.emit("dve", lambda e: e.scalar_tensor_tensor(
            rows[0:1, 2 * FT:3 * FT], psr[0:1, 0:FT], 1.0 / 1024,
            rows[0:1, FT:2 * FT], ALU.mult, ALU.subtract),
            waits=[tss, tmu2], inc=("dve", 1))
        self.psrow_free[par] = tvar
        trc = p.emit("dve", lambda e: e.tensor_scalar_add(
            rows[0:1, 2 * FT:3 * FT], rows[0:1, 2 * FT:3 * FT], 1e-5), inc=("dve", 1))
        tln_ = p.emit("act", lambda e: e.activation(rows[0:1, 2 * FT:3 * FT],
                                                    rows[0:1, 2 * FT:3 * FT], AF.Ln),
                      waits=[trc], inc=("act", 1))
        tsq3 = p.emit("act", lambda e: e.activation(rows[0:1, 2 * FT:3 * FT],
                                                    rows[0:1, 2 * FT:3 * FT], AF.Exp,
                                                    scale=-0.5),
                      inc=("act", 1))
        tb1 = p.emit("pe", lambda e: e.matmul(psb[:, 0:FT], self.ones_f[0:1, :],
                                              rows[0:1, 0:FT], start=True, stop=True),
                     waits=[tmu, self.psbc_free[par]], inc=("pe", 1))
        tc1 = p.emit("dve", lambda e: e.tensor_copy(mub[:], psb[:, 0:FT]),
                     waits=[tb1], inc=("dve", 1))
        tb2 = p.emit("pe", lambda e: e.matmul(psb[:, 0:FT], self.ones_f[0:1, :],
                                              rows[0:1, 2 * FT:3 * FT], start=True, stop=True),
                     waits=[tsq3, tc1], inc=("pe", 1))
        tc2 = p.emit("dve", lambda e: e.tensor_copy(rsb[:], psb[:, 0:FT]),
                     waits=[tb2], inc=("dve", 1))
        self.psbc_free[par] = tc2
        tlast = None
        for c in range(8):
            ts_ = p.emit("dve", lambda e, c=c: e.tensor_sub(
                scr[:], src[:, c * FT:(c + 1) * FT], mub[:]),
                waits=[tc1, tc2] if c == 0 else (), inc=("dve", 1))
            if gammac is not None:
                tlast = p.emit("dve", lambda e, c=c: e.scalar_tensor_tensor(
                    dst[:, c * FT:(c + 1) * FT], scr[:], gammac[:, c:c + 1], rsb[:],
                    ALU.mult, ALU.mult), inc=("dve", 1))
            else:
                tlast = p.emit("dve", lambda e, c=c: e.tensor_mul(
                    dst[:, c * FT:(c + 1) * FT], scr[:], rsb[:]), inc=("dve", 1))
        return tlast

    def stage_A1(self):
        cfg, p, dr = self.cfg, self.p, self.dr
        S, T = cfg.S, cfg.T
        FT = 256
        NTT = T // FT
        sb = self.sb

        self.mk_ps("A")
        ps_row, ps_bc = self.ps_row, self.ps_bc
        w_in = sb("A_w_in", (128, 8 * 2048), BF16)
        w_gt = sb("A_w_gt", (128, 8 * 2048), BF16)
        w_ot = sb("A_w_ot", (128, 8 * 1024), BF16)
        p.dma("sp", w_in[:].rearrange("p (k l) -> p k l", l=2048), self.wb["w_in"].rearrange("k p l -> p k l"),
              "s_Aw", waits=self.cast_done["w_in"])
        p.dma("sp", w_gt[:].rearrange("p (k l) -> p k l", l=2048), self.wb["w_gates"].rearrange("k p l -> p k l"),
              "s_Aw", waits=self.cast_done["w_gates"])
        p.dma("sp", w_ot[:].rearrange("p (k l) -> p k l", l=1024), self.wb["w_out"].rearrange("k p l -> p k l"),
              "s_Aw", waits=self.cast_done["w_out"])
        wA = p.tok("s_Aw")

        xb = [sb(f"A_xb{i}", (128, 8 * FT)) for i in range(2)]
        xbf = [sb(f"A_xbf{i}", (128, 8 * FT), BF16) for i in range(2)]
        sq = sb("A_sq", (128, 8 * FT), BF16)
        z0h = sb("A_z0h", (128, 8 * (FT + 3)))
        z0v = z0h[:].rearrange("p (c f) -> p c f", f=FT + 3)
        zb_ = sb("A_z", (128, 8 * FT))
        sig = sb("A_sig", (128, 8 * FT))
        alp = sb("A_alp", (128, 8 * FT))
        t1b = sb("A_t1", (128, 8 * FT))
        hbuf = sb("A_h", (128, 8 * FT))
        hst = sb("A_hst", (128, 8))
        zbf = sb("A_zbf", (128, 8 * FT), BF16)
        ggb = sb("A_ggb", (128, 8 * FT), BF16)
        hgb = [sb("A_hgb%d" % i, (128, 8 * FT), BF16) for i in range(2)]
        x1buf = sb("A_x1", (128, 8 * FT))
        rrow = sb("A_rrow", (1, 2 * FT))
        rinvb = [sb(f"A_rinvb{i}", (128, 2 * FT)) for i in range(2)]

        psg = self.ps_g
        ps_o = self.psum("A_ps_o", (128, 512))
        psg_free = self.psg_free
        pso_free = None
        xfree = [None, None]
        xbf_free = [None, None]
        zstore_last = []
        pend = None

        def proj(wt, rhs_bf, rhs_tok, ngrp, kcs, kcw, evac):
            for grp in range(ngrp):
                gp = grp % 2
                tmm = None
                for dd in range(2):
                    d = grp * 2 + dd
                    for kc in range(kcs):
                        tmm = p.emit("pe", lambda e, gp=gp, dd=dd, d=d, kc=kc: e.matmul(
                            psg[gp][:, dd * FT:(dd + 1) * FT],
                            wt[:, kc * kcw + d * 128: kc * kcw + (d + 1) * 128],
                            rhs_bf[:, kc * FT:(kc + 1) * FT],
                            start=(kc == 0), stop=(kc == kcs - 1)),
                            waits=([rhs_tok, psg_free[gp], wA] if (kc == 0 and dd == 0) else ()),
                            inc=("pe", 1) if (dd == 1 and kc == kcs - 1) else None)
                psg_free[gp] = evac(grp, psg[gp], tmm)

        NTOT = S * NTT
        casts = {}

        def stage0(i):
            s0, tt0 = divmod(i, NTT)
            par0 = i % 2
            xsrc = self.dfm(dr["x"][s0][:, :, tt0 * FT:(tt0 + 1) * FT])
            tload = p.dma("sp", xb[par0][:].rearrange("p (c f) -> p c f", f=FT),
                          xsrc, f"s_Ax{par0}", waits=[xfree[par0]])
            self.tap("xin", xb[par0], tload, s0, tt0, FT)
            p.emit("act", lambda e, par0=par0: e.copy(xbf[par0][:], xb[par0][:]),
                   waits=[tload, xbf_free[par0]], inc=("act", 1))
            casts[i] = p.emit("act", lambda e, par0=par0: e.activation(
                sq[:], xbf[par0][:], AF.Square),
                waits=[self.sq_free], inc=("act", 1))
        stage0(0)
        for s in range(S):
            for tt in range(NTT):
                i = s * NTT + tt
                par = i % 2
                gtoks = self.guard
                self.guard = []
                tcast = casts[i]
                tsq = tcast
                tsums = None
                for kc in range(8):
                    tsums = p.emit("pe", lambda e, kc=kc, par=par: e.matmul(
                        ps_row[par][0:1, 0:FT], self.ones_b[:],
                        sq[:, kc * FT:(kc + 1) * FT], start=(kc == 0), stop=(kc == 7)),
                        waits=([tsq, self.psrow_free[par], self.ones_tok, wA] if kc == 0 else ()),
                        inc=("pe", 1) if kc == 7 else None)
                self.sq_free = tsums
                trec = p.emit("act", lambda e, par=par: e.activation(
                    rrow[0:1, 0:FT], ps_row[par][0:1, 0:FT], AF.Ln),
                    waits=[tsums] + gtoks, inc=("act", 1))
                self.psrow_free[par] = trec
                tsq2 = p.emit("act", lambda e: e.activation(rrow[0:1, 0:FT],
                                                            rrow[0:1, 0:FT], AF.Exp,
                                                            scale=-0.5),
                              inc=("act", 1))
                tdup = p.emit("act", lambda e: e.copy(rrow[0:1, FT:2 * FT], rrow[0:1, 0:FT]),
                              inc=("act", 1))
                tbc = p.emit("pe", lambda e, par=par: e.matmul(
                    ps_bc[par][:], self.ones_f[0:1, :], rrow[0:1, :],
                    start=True, stop=True),
                    waits=[tdup, self.psbc_free[par]], inc=("pe", 1))
                trb = p.emit("dve", lambda e, par=par: e.tensor_copy(
                    rinvb[par][:], ps_bc[par][:]), waits=[tbc], inc=("dve", 1))
                self.psbc_free[par] = trb

                if tt > 0:
                    halo = p.emit("dve", lambda e: e.tensor_copy(
                        z0v[:, :, 0:3], z0v[:, :, FT:FT + 3]), inc=("dve", 1))
                else:
                    halo = p.emit("dve", lambda e: e.memset(z0v[:, :, 0:3], 0.0),
                                  inc=("dve", 1))

                conv_started = []
                def evac_u(grp, psv, tmm, par=par, trb=trb, halo=halo):
                    if grp < 4:
                        tv = p.emit("dve", lambda e, grp=grp, psv=psv: e.tensor_mul(
                            t1b[:, grp * 2 * FT:(grp + 1) * 2 * FT], psv[:], rinvb[par][:]),
                            waits=[tmm, trb], inc=("dve", 1))
                        p.emit("act", lambda e, grp=grp: e.activation(
                            ggb[:, grp * 2 * FT:(grp + 1) * 2 * FT],
                            t1b[:, grp * 2 * FT:(grp + 1) * 2 * FT], AF.Gelu),
                            waits=[tv], inc=("act", 1))
                        return tv
                    c0 = (grp - 4) * 2
                    tv = p.emit("dve", lambda e, c0=c0, psv=psv: e.tensor_mul(
                        z0v[:, c0:c0 + 2, 3:FT + 3],
                        psv[:].rearrange("p (c f) -> p c f", f=FT),
                        rinvb[par][:].rearrange("p (c f) -> p c f", f=FT)),
                        waits=[tmm, trb, halo], inc=("dve", 1))
                    for c in (c0, c0 + 1):
                        base = c * (FT + 3)
                        zv = zb_[:, c * FT:(c + 1) * FT]
                        p.emit("dve", lambda e, base=base, zv=zv, c=c: e.tensor_scalar(
                            zv, z0h[:, base + 3: base + 3 + FT],
                            self.convw[:, c * 4 + 3:c * 4 + 4],
                            self.convb[:, c:c + 1], ALU.mult, ALU.add),
                            waits=[self.small_tok] + zstore_last + gtoks)
                        for k in range(3):
                            tv = p.emit("dve", lambda e, base=base, zv=zv, c=c, k=k: e.scalar_tensor_tensor(
                                zv, z0h[:, base + k: base + k + FT],
                                self.convw[:, c * 4 + k:c * 4 + k + 1],
                                zv, ALU.mult, ALU.add))
                    conv_started.append(tv)
                    return tv
                proj(w_in, xbf[par], tcast, 8, 8, 2048, evac_u)
                tz0 = p.tok("dve")
                if "z0" in self.dbg:
                    tz0d = p.emit("dve", lambda e: e.tensor_copy(
                        t1b[:].rearrange("p (c f) -> p c f", f=FT),
                        z0v[:, :, 3:FT + 3]), waits=[tz0, p.tok("act")], inc=("dve", 1))
                    self.tap("z0", t1b, tz0d, s, tt, FT)
                tap_z0 = self.guard[-1] if "z0" in self.dbg else None
                xbf_free[par] = p.tok("pe")

                tconv = conv_started[-1]
                tzb = p.emit("act", lambda e: e.copy(zbf[:], zb_[:]),
                             waits=[tconv], inc=("act", 1))
                tap_zc = self.tap("zc", zb_, tconv, s, tt, FT)

                def evac_fg(grp, psv, tmm):
                    dstb = alp if grp < 4 else sig
                    coff = (grp % 4) * 2
                    tv = None
                    for dd in range(2):
                        d = grp * 2 + dd
                        c = coff + dd
                        tv = p.emit("act", lambda e, psv=psv, dd=dd, c=c, d=d, dstb=dstb: e.activation(
                            dstb[:, c * FT:(c + 1) * FT], psv[:, dd * FT:(dd + 1) * FT],
                            AF.Sigmoid, bias=self.gatesb[:, d:d + 1]),
                            waits=[tmm], inc=("act", 1))
                    return tv
                proj(w_gt, zbf, tzb, 8, 8, 2048, evac_fg)
                tsig = p.tok("act")

                ta1 = None
                for c in range(8):
                    ta1 = p.emit("dve", lambda e, c=c: e.tensor_scalar_mul(
                        alp[:, c * FT:(c + 1) * FT], alp[:, c * FT:(c + 1) * FT],
                        self.acoef[:, c:c + 1]),
                        waits=([tsig, self.acoef_tok] if c == 0 else ()),
                        inc=("dve", 1) if c == 7 else None)
                texp = p.emit("act", lambda e: e.activation(alp[:], alp[:], AF.Exp),
                              waits=[ta1], inc=("act", 1))
                self.tap("alp", alp, texp, s, tt, FT)
                self.tap("sig", sig, p.tok("act"), s, tt, FT)
                tt1 = p.emit("dve", lambda e: e.tensor_mul(t1b[:], alp[:], alp[:]),
                             waits=[texp, p.tok("act"), tap_z0], inc=("dve", 1))
                tt2a = p.emit("act", lambda e: e.activation(t1b[:], t1b[:], AF.Ln,
                                                            scale=-1.0, bias=self.c1eps[:]),
                              waits=[tt1], inc=("act", 1))
                tt2 = p.emit("act", lambda e: e.activation(t1b[:], t1b[:], AF.Exp,
                                                           scale=0.5),
                             inc=("act", 1))
                p.emit("dve", lambda e: e.tensor_mul(t1b[:], t1b[:], sig[:]),
                       waits=[tt2], inc=("dve", 1))
                tt4 = p.emit("dve", lambda e: e.tensor_mul(t1b[:], t1b[:], zb_[:]),
                             inc=("dve", 1))
                self.tap("xs", t1b, tt4, s, tt, FT)

                for c in range(8):
                    init = 0.0 if tt == 0 else hst[:, c:c + 1]
                    p.emit("dve", lambda e, c=c, init=init: e.tensor_tensor_scan(
                        hbuf[:, c * FT:(c + 1) * FT], alp[:, c * FT:(c + 1) * FT],
                        t1b[:, c * FT:(c + 1) * FT], init, ALU.mult, ALU.add),
                        waits=([tt4] if c == 0 else ()),
                        inc=("dve", 1) if c == 7 else None)
                p.emit("dve", lambda e: e.tensor_copy(
                    hst[:], hbuf[:].rearrange("p (c f) -> p c f", f=FT)[:, :, FT - 1:FT]),
                    inc=("dve", 1))
                thg = p.emit("dve", lambda e, par=par: e.tensor_mul(hgb[par][:], hbuf[:], ggb[:]),
                             inc=("dve", 1))
                self.tap("h", hbuf, p.tok("dve"), s, tt, FT)

                def stageB(ctx):
                    nonlocal pso_free, zstore_last
                    s_, tt_, par_, thg_ = ctx
                    tx1 = None
                    for d in range(8):
                        tmm = None
                        for kc in range(8):
                            tmm = p.emit("pe", lambda e, d=d, kc=kc, par_=par_: e.matmul(
                                ps_o[:, 0:FT],
                                w_ot[:, kc * 1024 + d * 128: kc * 1024 + (d + 1) * 128],
                                hgb[par_][:, kc * FT:(kc + 1) * FT],
                                start=(kc == 0), stop=(kc == 7)),
                                waits=([thg_, pso_free, wA] if kc == 0 else ()),
                                inc=("pe", 1) if kc == 7 else None)
                        pso_free = p.emit("dve", lambda e, d=d, par_=par_: e.tensor_add(
                            x1buf[:, d * FT:(d + 1) * FT], ps_o[:, 0:FT],
                            xb[par_][:, d * FT:(d + 1) * FT]),
                            waits=[tmm] + zstore_last, inc=("dve", 1))
                    tx1 = pso_free
                    xfree[par_] = tx1
                    zst = p.dma("sp", self.dfm(self.x1_fm[s_][:, :, tt_ * FT:(tt_ + 1) * FT]),
                                x1buf[:].rearrange("p (c f) -> p c f", f=FT),
                                f"s_Az{par_}", waits=[tx1])
                    zstore_last = [zst]
                    if "x1" in self.dbg:
                        zstore_last.append(
                            p.dma("sp", self.dfm(self.dbg["x1"][s_][:, :, tt_ * FT:(tt_ + 1) * FT]),
                                  x1buf[:].rearrange("p (c f) -> p c f", f=FT), "s_dbg",
                                  waits=[tx1]))
                if pend is not None:
                    stageB(pend)
                pend = (s, tt, par, thg)
                if i + 1 < NTOT:
                    stage0(i + 1)
        stageB(pend)
        self.a1_slots = ["s_Az0", "s_Az1", "s_Ax0", "s_Ax1", "s_Aw"]

    def stage_A2(self):
        cfg, p, dr = self.cfg, self.p, self.dr
        S, T = cfg.S, cfg.T
        FT = 256
        NTT = T // FT
        NTOT = S * NTT
        sb = self.sb

        self.mk_ps("G")
        ps_row, ps_bc = self.ps_row, self.ps_bc
        w_gr = sb("G_w_gr", (128, 8 * 4096), BF16)
        w_sh = sb("G_w_sh", (128, 16 * 1024), BF16)
        p.dma("sp", w_gr[:].rearrange("p (k l) -> p k l", l=4096),
              self.wb["w_grow"].rearrange("k p l -> p k l"), "s_Gw",
              waits=self.cast_done["w_grow"])
        p.dma("sp", w_sh[:].rearrange("p (k l) -> p k l", l=1024),
              self.wb["w_shrink"].rearrange("k p l -> p k l"), "s_Gw",
              waits=self.cast_done["w_shrink"])
        wG = p.tok("s_Gw")

        x1b = [sb("G_x1b%d" % i, (128, 8 * FT)) for i in range(2)]
        x1f = [sb("G_x1f%d" % i, (128, 8 * FT), BF16) for i in range(2)]
        xob = [sb("G_xob%d" % i, (128, 8 * FT)) for i in range(2)]
        sq = sb("G_sq", (128, 8 * FT), BF16)
        g2b = [sb("G_g2b%d" % i, (128, 16 * FT), BF16) for i in range(2)]
        tg = sb("G_tg", (128, FT))
        tv_ = sb("G_tv", (128, FT))
        sbs = sb("G_s", (128, 8 * FT))
        y1 = sb("G_y1", (128, 8 * FT))
        ainb = sb("G_ain", (128, 8 * FT), BF16)
        rrow = sb("G_rrow", (1, 2 * FT))
        rows3 = sb("G_rows3", (1, 3 * FT))
        rinvb = [sb("G_rinvb%d" % i, (128, 2 * FT)) for i in range(2)]
        mub = sb("G_mub", (128, FT))
        rsb = sb("G_rsb", (128, FT))
        scr = sb("G_scr", (128, FT))

        psg = self.ps_g
        psg_free = self.psg_free
        ps_o = self.psum("G_ps_o", (128, 512))
        pso_free = None
        x1free = [None, None]
        x1f_free = [None, None]
        xofree = [None, None]
        casts = {}
        loadx = {}
        pend = None

        def stage0(i):
            s0, tt0 = divmod(i, NTT)
            par0 = i % 2
            tl1 = p.dma("sp", x1b[par0][:].rearrange("p (c f) -> p c f", f=FT),
                        self.dfm(self.x1_fm[s0][:, :, tt0 * FT:(tt0 + 1) * FT]),
                        "s_Gx1%d" % par0,
                        waits=[x1free[par0],
                               ("s_Az%d" % par0, 16 * (i // 2 + 1)) if "A" in cfg.stages else None])
            loadx[i] = p.dma("sp", xob[par0][:].rearrange("p (c f) -> p c f", f=FT),
                             self.dfm(dr["x"][s0][:, :, tt0 * FT:(tt0 + 1) * FT]),
                             "s_Gx%d" % par0, waits=[xofree[par0]])
            p.emit("act", lambda e, par0=par0: e.copy(x1f[par0][:], x1b[par0][:]),
                   waits=[tl1, x1f_free[par0]], inc=("act", 1))
            casts[i] = p.emit("act", lambda e, par0=par0: e.activation(
                sq[:], x1f[par0][:], AF.Square),
                waits=[self.sq_free], inc=("act", 1))

        def stageB(ctx):
            nonlocal pso_free
            i_, s_, tt_, par_, tg2_ = ctx
            tlast = None
            for d in range(8):
                tmm = None
                for kc in range(16):
                    tmm = p.emit("pe", lambda e, d=d, kc=kc, par_=par_: e.matmul(
                        ps_o[:, 0:FT],
                        w_sh[:, kc * 1024 + d * 128: kc * 1024 + (d + 1) * 128],
                        g2b[par_][:, kc * FT:(kc + 1) * FT],
                        start=(kc == 0), stop=(kc == 15)),
                        waits=([tg2_, pso_free, wG] if kc == 0 else ()),
                        inc=("pe", 1) if (kc == 15) else None)
                t_ = p.emit("dve", lambda e, d=d, par_=par_: e.tensor_add(
                    sbs[:, d * FT:(d + 1) * FT], ps_o[:, 0:FT],
                    x1b[par_][:, d * FT:(d + 1) * FT]),
                    waits=[tmm] + self.guard, inc=("dve", 1))
                self.guard = []
                tlast = p.emit("dve", lambda e, d=d, par_=par_: e.tensor_add(
                    sbs[:, d * FT:(d + 1) * FT], sbs[:, d * FT:(d + 1) * FT],
                    xob[par_][:, d * FT:(d + 1) * FT]),
                    waits=[loadx[i_]], inc=("dve", 1))
                pso_free = tlast
            x1free[par_] = tlast
            xofree[par_] = tlast

            ty1 = self.ln_fm(sbs, y1, None, par_, FT, sq, rows3, mub, rsb, scr,
                             tlast)
            t1 = p.dma("sp", self.dfm(self.y1_fm[s_][:, :, tt_ * FT:(tt_ + 1) * FT]),
                       y1[:].rearrange("p (c f) -> p c f", f=FT), "s_Gy%d" % par_,
                       waits=[ty1])
            self.guard.append(t1)
            if "y1" in self.dbg:
                t_ = p.dma("sp", self.dfm(self.dbg["y1"][s_][:, :, tt_ * FT:(tt_ + 1) * FT]),
                           y1[:].rearrange("p (c f) -> p c f", f=FT), "s_dbg2",
                           waits=[ty1])
                self.guard.append(t_)
            tain = self.ln_fm(y1, ainb, self.ln1c, par_, FT, sq, rows3, mub, rsb,
                              scr, ty1)
            t2 = p.dma("sp", self.dfm(self.ain_d[s_][:, :, tt_ * FT:(tt_ + 1) * FT]),
                       ainb[:].rearrange("p (c f) -> p c f", f=FT), "s_Ga%d" % par_,
                       waits=[tain])
            self.guard.append(t2)
            if "ain" in self.dbg:
                tup = p.emit("dve", lambda e: e.tensor_copy(sbs[:], ainb[:]),
                             waits=[tain], inc=("dve", 1))
                t_ = p.dma("sp", self.dfm(self.dbg["ain"][s_][:, :, tt_ * FT:(tt_ + 1) * FT]),
                           sbs[:].rearrange("p (c f) -> p c f", f=FT), "s_dbg3",
                           waits=[tup])
                self.guard.append(t_)

        stage0(0)
        for s in range(S):
            for tt in range(NTT):
                i = s * NTT + tt
                par = i % 2
                tc1 = casts[i]
                tsq = tc1
                tsums = None
                for kc in range(8):
                    tsums = p.emit("pe", lambda e, kc=kc, par=par: e.matmul(
                        ps_row[par][0:1, 0:FT], self.ones_b[:],
                        sq[:, kc * FT:(kc + 1) * FT], start=(kc == 0), stop=(kc == 7)),
                        waits=([tsq, self.psrow_free[par], wG] if kc == 0 else ()),
                        inc=("pe", 1) if kc == 7 else None)
                self.sq_free = tsums
                trec = p.emit("act", lambda e, par=par: e.activation(
                    rrow[0:1, 0:FT], ps_row[par][0:1, 0:FT], AF.Ln),
                    waits=[tsums], inc=("act", 1))
                self.psrow_free[par] = trec
                p.emit("act", lambda e: e.activation(rrow[0:1, 0:FT], rrow[0:1, 0:FT],
                                                     AF.Exp, scale=-0.5),
                       inc=("act", 1))
                tdup = p.emit("act", lambda e: e.copy(rrow[0:1, FT:2 * FT], rrow[0:1, 0:FT]),
                              inc=("act", 1))
                tbc = p.emit("pe", lambda e, par=par: e.matmul(
                    ps_bc[par][:], self.ones_f[0:1, :], rrow[0:1, :],
                    start=True, stop=True),
                    waits=[tdup, self.psbc_free[par]], inc=("pe", 1))
                trb = p.emit("dve", lambda e, par=par: e.tensor_copy(
                    rinvb[par][:], ps_bc[par][:]), waits=[tbc], inc=("dve", 1))
                self.psbc_free[par] = trb

                tg2 = None
                for c in range(16):
                    gp = c % 2
                    tmm = None
                    for dd, d in enumerate((c, 16 + c)):
                        for kc in range(8):
                            tmm = p.emit("pe", lambda e, gp=gp, dd=dd, d=d, kc=kc, par=par: e.matmul(
                                psg[gp][:, dd * FT:(dd + 1) * FT],
                                w_gr[:, kc * 4096 + d * 128: kc * 4096 + (d + 1) * 128],
                                x1f[par][:, kc * FT:(kc + 1) * FT],
                                start=(kc == 0), stop=(kc == 7)),
                                waits=([tc1, psg_free[gp], wG] if (kc == 0 and dd == 0) else ()),
                                inc=("pe", 1) if (dd == 1 and kc == 7) else None)
                    ta = p.emit("dve", lambda e, gp=gp, par=par: e.tensor_mul(
                        tg[:], psg[gp][:, 0:FT], rinvb[par][:, 0:FT]),
                        waits=[tmm, trb], inc=("dve", 1))
                    tb = p.emit("dve", lambda e, gp=gp, par=par: e.tensor_mul(
                        tv_[:], psg[gp][:, FT:2 * FT], rinvb[par][:, 0:FT]),
                        inc=("dve", 1))
                    tgl = p.emit("act", lambda e: e.activation(tg[:], tg[:], AF.Gelu),
                                 waits=[ta], inc=("act", 1))
                    tg2 = p.emit("dve", lambda e, c=c, par=par: e.tensor_mul(
                        g2b[par][:, c * FT:(c + 1) * FT], tg[:], tv_[:]),
                        waits=[tgl], inc=("dve", 1))
                    psg_free[gp] = tg2
                x1f_free[par] = p.tok("pe")

                if pend is not None:
                    stageB(pend)
                pend = (i, s, tt, par, tg2)
                if i + 1 < NTOT:
                    stage0(i + 1)
        stageB(pend)
        self.a2_slots = ["s_Gx10", "s_Gx11", "s_Gx0", "s_Gx1", "s_Gy0", "s_Gy1",
                         "s_Ga0", "s_Ga1", "s_Gw"]

    def stage_B1(self):
        cfg, p, dr = self.cfg, self.p, self.dr
        S, T = cfg.S, cfg.T
        FT = 512
        NTT = T // FT
        sb = self.sb

        self.mk_ps("M", row=False, bc=False)
        wq1 = sb("M_wq1", (128, 2 * 1024), BF16)
        wq2 = sb("M_wq2", (128, 2 * 1024), BF16)
        wk1 = sb("M_wk1", (128, 8 * 1024), BF16)
        wk2 = sb("M_wk2", (128, 8 * 1024), BF16)
        wkvd = sb("M_wkvd", (128, 8 * 256), BF16)
        wqd = sb("M_wqd", (128, 8 * 256), BF16)
        wvu = sb("M_wvu", (128, 2 * 1024), BF16)
        cosM = sb("M_cos", (128, T))
        sinM = sb("M_sin", (128, T))
        p.dma("sp", wq1[:].rearrange("p (k l) -> p k l", l=1024),
              self.wb["w_q1"].rearrange("k p l -> p k l"), "s_Mw",
              waits=self.cast_done["w_q1"])
        p.dma("sp", wq2[:].rearrange("p (k l) -> p k l", l=1024),
              self.wb["w_q2"].rearrange("k p l -> p k l"), "s_Mw",
              waits=self.cast_done["w_q2"] + [self.fix_done])
        p.dma("sp", wk1[:].rearrange("p (k l) -> p k l", l=1024),
              self.wb["w_k1"].rearrange("k p l -> p k l"), "s_Mw",
              waits=self.cast_done["w_k1"] + [self.k1_done])
        p.dma("sp", wk2[:].rearrange("p (k l) -> p k l", l=1024),
              self.wb["w_k2"].rearrange("k p l -> p k l"), "s_Mw",
              waits=self.cast_done["w_k2"] + [self.fix_done])
        p.dma("sp", wkvd[:].rearrange("p (k l) -> p k l", l=256),
              self.wb["w_kvd"].rearrange("k p l -> p k l"), "s_Mw",
              waits=self.cast_done["w_kvd"])
        p.dma("sp", wqd[:].rearrange("p (k l) -> p k l", l=256),
              self.wb["w_qd"].rearrange("k p l -> p k l"), "s_Mw",
              waits=self.cast_done["w_qd"])
        p.dma("sp", wvu[:].rearrange("p (k l) -> p k l", l=1024),
              self.wb["w_vu"].rearrange("k p l -> p k l"), "s_Mw",
              waits=self.cast_done["w_vu"])
        p.dma("sp", cosM[:], dr["c_cos"], "s_Mw")
        p.dma("sp", sinM[:], dr["c_sin"], "s_Mw")
        wM = p.tok("s_Mw")

        ainb = [sb("M_ain%d" % i, (128, 8 * FT), BF16) for i in range(2)]
        kvl = sb("M_kvl", (128, 2 * FT), BF16)
        ql = sb("M_ql", (128, 2 * FT), BF16)
        scr1 = sb("M_scr1", (128, FT))
        scr2 = sb("M_scr2", (128, FT))
        qkout = sb("M_qk", (128, 8 * FT), BF16)
        vout = sb("M_vout", (128, 1024), BF16)

        psg = self.ps_g
        psg_free = self.psg_free
        ainfree = [None, None]
        qkout_free = [None]
        vfree = [None]
        latfree = [None]

        for s in range(S):
            for tt in range(NTT):
                i = s * NTT + tt
                par = i % 2
                tla = p.dma("sp", ainb[par][:].rearrange("p (c f) -> p c f", f=FT),
                            self.dfm(self.ain_d[s][:, :, tt * FT:(tt + 1) * FT]),
                            "s_Ma%d" % par, waits=[ainfree[par]])

                def lat(wt, dstb, rhs_tok):
                    tlast = None
                    for d in range(2):
                        gp = d % 2
                        tmm = None
                        for kc in range(8):
                            tmm = p.emit("pe", lambda e, gp=gp, d=d, kc=kc, par=par, wt=wt: e.matmul(
                                psg[gp][:, 0:FT],
                                wt[:, kc * 256 + d * 128: kc * 256 + (d + 1) * 128],
                                ainb[par][:, kc * FT:(kc + 1) * FT],
                                start=(kc == 0), stop=(kc == 7)),
                                waits=([rhs_tok, psg_free[gp], wM] if kc == 0 else ()),
                                inc=("pe", 1) if kc == 7 else None)
                        tlast = p.emit("dve", lambda e, gp=gp, d=d, dstb=dstb: e.tensor_copy(
                            dstb[:, d * FT:(d + 1) * FT], psg[gp][:, 0:FT]),
                            waits=[tmm, latfree[0]], inc=("dve", 1))
                        psg_free[gp] = tlast
                    return tlast
                tkv = lat(wkvd, kvl, tla)
                tql = lat(wqd, ql, tla)

                def qk_tiles(w1, w2, rhsb, nkc, rhs_tok, dst_d, slot, extra_w):
                    for g in range(8):
                        tm1 = tm2 = None
                        for kc in range(nkc):
                            tm1 = p.emit("pe", lambda e, g=g, kc=kc, w1=w1, rhsb=rhsb: e.matmul(
                                psg[0][:, 0:FT],
                                w1[:, kc * 1024 + g * 128: kc * 1024 + (g + 1) * 128],
                                rhsb[:, kc * FT:(kc + 1) * FT],
                                start=(kc == 0), stop=(kc == nkc - 1)),
                                waits=([rhs_tok, psg_free[0], wM] if kc == 0 else ()),
                                inc=("pe", 1) if kc == nkc - 1 else None)
                        for kc in range(nkc):
                            tm2 = p.emit("pe", lambda e, g=g, kc=kc, w2=w2, rhsb=rhsb: e.matmul(
                                psg[1][:, 0:FT],
                                w2[:, kc * 1024 + g * 128: kc * 1024 + (g + 1) * 128],
                                rhsb[:, kc * FT:(kc + 1) * FT],
                                start=(kc == 0), stop=(kc == nkc - 1)),
                                waits=([psg_free[1]] if kc == 0 else ()),
                                inc=("pe", 1) if kc == nkc - 1 else None)
                        t1 = p.emit("dve", lambda e, tt=tt: e.tensor_mul(
                            scr1[:], psg[0][:, 0:FT], cosM[:, tt * FT:(tt + 1) * FT]),
                            waits=[tm1, wM] + (extra_w if g == 0 else []),
                            inc=("dve", 1))
                        psg_free[0] = t1
                        t2 = p.emit("dve", lambda e, tt=tt: e.tensor_mul(
                            scr2[:], psg[1][:, 0:FT], sinM[:, tt * FT:(tt + 1) * FT]),
                            waits=[tm2], inc=("dve", 1))
                        psg_free[1] = t2
                        p.emit("dve", lambda e, g=g: e.tensor_add(
                            qkout[:, g * FT:(g + 1) * FT], scr1[:], scr2[:]),
                            inc=("dve", 1))
                    tq = p.tok("dve")
                    return p.dma("sp", self.dfm(dst_d[s][:, :, tt * FT:(tt + 1) * FT]),
                                 qkout[:].rearrange("p (c f) -> p c f", f=FT),
                                 slot, waits=[tq])
                tqs = qk_tiles(wq1, wq2, ql, 2, tql, self.q_d, "s_MqS%d" % par,
                               qkout_free)
                tks = qk_tiles(wk1, wk2, ainb[par], 8, tla, self.k_dd, "s_MkS%d" % par,
                               [tqs])
                qkout_free = [tks]

                for tk4 in range(FT // 128):
                    tmm = None
                    for half in range(2):
                        for lc in range(2):
                            tmm = p.emit("pe", lambda e, tk4=tk4, half=half, lc=lc: e.matmul(
                                psg[half][:, 0:FT],
                                kvl[:, lc * FT + tk4 * 128: lc * FT + tk4 * 128 + 128],
                                wvu[:, lc * 1024 + half * 512: lc * 1024 + (half + 1) * 512],
                                start=(lc == 0), stop=(lc == 1)),
                                waits=([tkv, psg_free[half]] if lc == 0 else ()),
                                inc=("pe", 1) if lc == 1 else None)
                    tvc = p.emit("dve", lambda e: e.tensor_copy(vout[:, 0:512], psg[0][:, 0:FT]),
                                 waits=[tmm, vfree[0]], inc=("dve", 1))
                    psg_free[0] = tvc
                    tvc2 = p.emit("dve", lambda e: e.tensor_copy(vout[:, 512:1024], psg[1][:, 0:FT]),
                                  inc=("dve", 1))
                    psg_free[1] = tvc2
                    tv = p.dma("sp", self.v_dd[s][tt * 4 + tk4], vout[:],
                               "s_Mv%d" % par, waits=[tvc2])
                    vfree = [tv]
                ainfree[par] = p.tok("pe")
                latfree = [p.tok("pe")]

    def stage_B2(self):
        cfg, p, dr = self.cfg, self.p, self.dr
        S, T = cfg.S, cfg.T
        FT = 512
        NQT = T // FT
        sb = self.sb

        self.ps_row = [self.psum("T_row", (1, 512))]
        self.ps_bc = [self.psum("T_bc", (128, 512))]
        self.psrow_free = [None]
        self.psbc_free = [None]
        self.sq_free = None
        ps_s = self.psum("T_ps_s", (128, 2048))
        ps_t = self.psum("T_ps_t", (128, 512), BF16)
        ps_pv = self.psum("T_ps_pv", (128, 512))

        w_o = sb("T_wo", (128, 8 * 1024), BF16)
        caus = sb("T_caus", (128, 896))
        ident = sb("T_id", (128, 128), BF16)
        p.dma("sp", w_o[:].rearrange("p (k l) -> p k l", l=1024),
              self.wb["w_o"].rearrange("k p l -> p k l"), "s_Tw",
              waits=self.cast_done["w_o"])
        p.dma("sp", caus[:], dr["c_caus"], "s_Tw")
        p.dma("sp", ident[:], dr["c_ident"], "s_Tw")
        wT = p.tok("s_Tw")

        Ks = sb("T_K", (128, 8 * T), BF16)
        Vs = sb("T_V", (128, 16 * 1024), BF16)
        Q1 = sb("T_Q", (128, 8 * FT), BF16)
        Qs = [Q1, Q1]
        y1b_ = sb("T_y1", (128, 8 * FT))
        y1b = [y1b_, y1b_]
        Abf = sb("T_Abf", (128, 2048), BF16)
        AT = sb("T_AT", (128, 16 * 512), BF16)
        obf = sb("T_obf", (128, 8 * FT), BF16)
        y2 = sb("T_y2", (128, 8 * FT))
        rbuf = y1b_
        minb = AT[:, 8 * FT:16 * FT]
        sqb = AT[:, 0:8 * FT]
        sc = [sb("T_sc%d" % i, (128, 2048)) for i in range(3)]
        rows3 = sc[1][0:1, 0:3 * FT]
        Ab2 = [Abf, sb("T_Abf2", (128, 2048), BF16)]
        mub = sc[0][:, 0:FT]
        rsb = sc[0][:, FT:2 * FT]
        scr = sc[0][:, 2 * FT:3 * FT]
        nmax = [sb("T_nmax%d" % i, (128, 1)) for i in range(2)]
        rsum = [sb("T_rsum%d" % i, (128, 1)) for i in range(2)]

        qfree = [None, None]
        ps_free = {}
        sc_free = [[], [], []]
        nmax_free = [None, None]
        abf_free2 = [None, None]
        ps_t_free = None
        ps_pv_free = None
        at_free = []
        o_free = None
        abf_free = None
        kv_free = None
        y2_free = None
        min_store = None
        r_store = None

        for s in range(S):
            kload = p.dma("sp", Ks[:].rearrange("p (c f) -> p c f", f=T),
                          self.dfm(self.k_dd[s]), "s_TK", waits=[kv_free])
            p.dma("sp", Vs[:, 0:(T // 128) * 1024].rearrange("p (c f) -> p c f", f=1024),
                  self.v_dd[s].rearrange("c p f -> p c f"), "s_TK")
            kv_tok = p.tok("s_TK")
            for qt in range(NQT):
                i = s * NQT + qt
                par = i % 2
                tlq = p.dma("sp", Q1[:].rearrange("p (c f) -> p c f", f=FT),
                            self.dfm(self.q_d[s][:, :, qt * FT:(qt + 1) * FT]),
                            "s_TQ", waits=[qfree[0]])
                tly = p.dma("sp", y1b_[:].rearrange("p (c f) -> p c f", f=FT),
                            self.dfm(self.y1_fm[s][:, :, qt * FT:(qt + 1) * FT]),
                            "s_Ty", waits=[r_store])
                kts = qt + 1
                W = kts * 512
                def do_scores(h, qj):
                    idx = h * 4 + qj
                    g, hh = h // 2, h % 2
                    st = idx % 3
                    scs = sc[st]
                    tmm = {}
                    for kt in range(kts):
                        tmm[kt] = p.emit("pe", lambda e, g=g, hh=hh, qj=qj, kt=kt, par=par: e.matmul(
                            ps_s[:, kt * 512:(kt + 1) * 512],
                            Qs[par][hh * 64:(hh + 1) * 64,
                                    g * FT + qj * 128: g * FT + qj * 128 + 128],
                            Ks[hh * 64:(hh + 1) * 64,
                               g * T + kt * 512: g * T + (kt + 1) * 512],
                            start=True, stop=True),
                            waits=([tlq, kv_tok, wT, ps_free.get(kt)]
                                   + (sc_free[st] if kt == 0 else [])),
                            inc=("pe", 1))
                    delta = qj * 128
                    evs = []
                    for kt in range(kts):
                        if kt == kts - 1:
                            tev = p.emit("dve", lambda e, kt=kt, delta=delta, scs=scs: e.tensor_add(
                                scs[:, kt * 512:(kt + 1) * 512],
                                ps_s[:, kt * 512:(kt + 1) * 512],
                                caus[:, 384 - delta: 896 - delta]),
                                waits=[tmm[kt], wT] + sc_free[st], inc=("dve", 1))
                        else:
                            tev = p.emit("act", lambda e, kt=kt, scs=scs: e.copy(
                                scs[:, kt * 512:(kt + 1) * 512],
                                ps_s[:, kt * 512:(kt + 1) * 512]),
                                waits=[tmm[kt]] + sc_free[st], inc=("act", 1))
                        ps_free[kt] = tev
                        evs.append(tev)
                    return (h, qj, st, evs)

                def do_max(ctx):
                    h, qj, st, evs = ctx
                    st2 = (h * 4 + qj) % 2
                    nmx = nmax[st2]
                    tmax = p.emit("dve", lambda e, W=W, nmx=nmx, scs=sc[st]: e.reduce_max(
                        nmx[:], scs[:, 0:W], axis=mybir.AxisListType.X,
                        negate=True), waits=evs + [nmax_free[st2]], inc=("dve", 1))
                    return ctx + (tmax,)

                def do_exp(ctx):
                    h, qj, st, evs, tmax = ctx
                    st2 = (h * 4 + qj) % 2
                    texp = p.emit("act", lambda e, W=W, Ax=Ab2[st2], scs=sc[st],
                                  nmx=nmax[st2], rsm=rsum[st2]: e.activation(
                        Ax[:, 0:W], scs[:, 0:W], AF.Exp, bias=nmx[:],
                        accum_out=rsm[:]),
                        waits=[tmax, abf_free2[st2]], inc=("act", 1))
                    sc_free[st] = [texp]
                    nmax_free[st2] = texp
                    return ctx + (texp,)

                def do_scale(ctx):
                    h, qj, st, evs, tmax, texp = ctx
                    st2 = (h * 4 + qj) % 2
                    trc = p.emit("dve", lambda e, rsm=rsum[st2]: e.reciprocal(rsm[:], rsm[:]),
                                 waits=[texp], inc=("dve", 1))
                    tsc = p.emit("dve", lambda e, W=W, Ax=Ab2[st2], rsm=rsum[st2]: e.tensor_scalar_mul(
                        Ax[:, 0:W], Ax[:, 0:W], rsm[:]),
                        waits=[trc], inc=("dve", 1))
                    return ctx + (tsc,)

                def do_trans(ctx):
                    nonlocal ps_t_free, at_free, abf_free2
                    h, qj, st, evs, tmax, texp, tsc = ctx
                    st2 = (h * 4 + qj) % 2
                    Ax = Ab2[st2]
                    ttr = None
                    for m4 in range(kts):
                        for j in range(4):
                            kb = m4 * 4 + j
                            ttr = p.emit("pe", lambda e, kb=kb, j=j, Ax=Ax: e.transpose(
                                ps_t[:, j * 128:(j + 1) * 128],
                                Ax[:, kb * 128:(kb + 1) * 128], ident[:]),
                                waits=([tsc, ps_t_free] if j == 0 else ()),
                                inc=("pe", 1) if j == 3 else None)
                        tev = p.emit("act", lambda e, m4=m4, qj=qj: e.copy(
                            AT[:].rearrange("p (kb f) -> p kb f", f=512)[
                                :, m4 * 4:(m4 + 1) * 4, qj * 128:(qj + 1) * 128],
                            ps_t[:].rearrange("p (j f) -> p j f", f=128)),
                            waits=[ttr] + at_free, inc=("act", 1))
                        ps_t_free = tev
                    abf_free2[st2] = ttr

                def do_pv(h):
                    nonlocal at_free, ps_pv_free, o_free
                    g, hh = h // 2, h % 2
                    tat = p.tok("act")
                    tpv = None
                    for kb in range(4 * kts):
                        tpv = p.emit("pe", lambda e, kb=kb, h=h, hh=hh, kts=kts: e.matmul(
                            ps_pv[hh * 64:(hh + 1) * 64, :],
                            Vs[:, kb * 1024 + h * 64: kb * 1024 + (h + 1) * 64],
                            AT[:, kb * 512:(kb + 1) * 512],
                            start=(kb == 0), stop=(kb == 4 * kts - 1)),
                            waits=([tat, ps_pv_free] if kb == 0 else ()),
                            inc=("pe", 1) if kb == 4 * kts - 1 else None)
                    at_free = [tpv]
                    if hh == 1:
                        tpe = p.emit("dve", lambda e, g=g: e.tensor_copy(
                            obf[:, g * FT:(g + 1) * FT], ps_pv[:]),
                            waits=[tpv, o_free], inc=("dve", 1))
                        ps_pv_free = tpe

                stages = [do_max, do_exp, do_scale, do_trans]
                pend = []
                for idx in range(64):
                    h, qj = divmod(idx, 4)
                    newp = []
                    done_trans = None
                    for si, ctx in pend:
                        r = stages[si](ctx)
                        if si + 1 < len(stages):
                            newp.append((si + 1, r))
                        else:
                            done_trans = ctx
                    pend = newp
                    if done_trans is not None and done_trans[1] == 3:
                        do_pv(done_trans[0])
                    pend.append((0, do_scores(h, qj)))
                while pend:
                    newp = []
                    done_trans = None
                    for si, ctx in pend:
                        r = stages[si](ctx)
                        if si + 1 < len(stages):
                            newp.append((si + 1, r))
                        else:
                            done_trans = ctx
                    pend = newp
                    if done_trans is not None and done_trans[1] == 3:
                        do_pv(done_trans[0])
                to_ = p.tok("dve")
                ev = {}
                for d in range(8):
                    sl = d % 4
                    tmm = None
                    for kc in range(8):
                        tmm = p.emit("pe", lambda e, d=d, sl=sl, kc=kc: e.matmul(
                            ps_s[:, sl * 512:(sl + 1) * 512],
                            w_o[:, kc * 1024 + d * 128: kc * 1024 + (d + 1) * 128],
                            obf[:, kc * FT:(kc + 1) * FT],
                            start=(kc == 0), stop=(kc == 7)),
                            waits=(([to_, ev.get(sl), y2_free, ps_free.get(sl)])
                                   if kc == 0 else ()),
                            inc=("pe", 1) if kc == 7 else None)
                    ev[sl] = p.emit("dve", lambda e, d=d, sl=sl, par=par: e.tensor_add(
                        y2[:, d * FT:(d + 1) * FT], ps_s[:, sl * 512:(sl + 1) * 512],
                        y1b[par][:, d * FT:(d + 1) * FT]),
                        waits=[tmm, tly, min_store], inc=("dve", 1))
                for sl in range(4):
                    ps_free[sl] = ev[sl]
                o_free = p.tok("pe")
                ty2 = p.tok("dve")
                tr = None
                for q4 in range(4):
                    tr = p.emit("dve", lambda e, q4=q4: e.tensor_add(
                        rbuf[:, q4 * 1024:(q4 + 1) * 1024],
                        y2[:, q4 * 1024:(q4 + 1) * 1024],
                        rbuf[:, q4 * 1024:(q4 + 1) * 1024]),
                        waits=[r_store, ty2], inc=("dve", 1))
                r_store = p.dma("sp", self.dfm(self.r_fm[s][:, :, qt * FT:(qt + 1) * FT]),
                                rbuf[:].rearrange("p (c f) -> p c f", f=FT),
                                "s_Tr%d" % par, waits=[tr])
                if "r" in self.dbg:
                    r_store = p.dma("sp", self.dfm(self.dbg["r"][s][:, :, qt * FT:(qt + 1) * FT]),
                                    rbuf[:].rearrange("p (c f) -> p c f", f=FT),
                                    "s_dbgr", waits=[tr])
                self.sq_free = at_free[0] if at_free else None
                tmin = self.ln_fm(y2, minb, self.ln2c, 0, FT, sqb, rows3, mub, rsb,
                                  scr, ty2)
                y2_free = tmin
                sc_free[0] = [tmin, p.tok("act")]
                sc_free[1] = sc_free[0]
                min_store = p.dma("sp", self.dfm(self.min_d[s][:, :, qt * FT:(qt + 1) * FT]),
                                  minb[:].rearrange("p (c f) -> p c f", f=FT),
                                  "s_Tm%d" % par, waits=[tmin])
                at_free = [tmin, min_store]
                if "y2" in self.dbg:
                    y2_free = p.dma("sp", self.dfm(self.dbg["y2"][s][:, :, qt * FT:(qt + 1) * FT]),
                                    y2[:].rearrange("p (c f) -> p c f", f=FT),
                                    "s_dbgy2", waits=[ty2])
                qfree[0] = p.tok("pe")
            kv_free = p.tok("pe")

    def stage_C(self):
        cfg, p, dr = self.cfg, self.p, self.dr
        S, T = cfg.S, cfg.T
        FT = 512
        HALF = 512
        NH_ = (S * T) // HALF
        NTL = HALF // FT
        sb = self.sb

        self.ps_row = [self.psum("C_row", (1, 512))]
        self.ps_bc = [self.psum("C_bc", (128, 512))]
        ps_bc2 = self.psum("C_bc2", (128, 512))
        self.psrow_free = [None]
        self.psbc_free = [None]
        self.sq_free = None
        ps_g = [self.psum("C_g%d" % i, (128, 512)) for i in range(2)]
        ps_d = [self.psum("C_d%d" % i, (128, 512)) for i in range(2)]
        ps_r = self.psum("C_r", (128, 128))

        bg = sb("C_bg", (128, 8 * 2048), BF16)
        bu = sb("C_bu", (128, 8 * 2048), BF16)
        bd = [sb("C_bd%d" % i, (128, 16 * 512), BF16) for i in range(2)]
        wrt = sb("C_wrt", (128, 8 * 7), BF16)
        minb = [sb("C_min0", (128, 8 * HALF), BF16), self.fxev8]
        acc = sb("C_acc", (128, 8 * HALF))
        tpb = sb("C_tp", (128, 16 * FT), BF16)
        sgs = sb("C_sgs", (128, FT))
        rbt = sb("C_rbt", (128, 8 * FT))
        rows3 = sb("C_rows3", (1, 3 * FT))
        probs = sb("C_probs", (128, 16))
        m1 = sb("C_m1", (128, 1))
        m2 = sb("C_m2", (128, 1))
        msk = sb("C_msk", (128, 16))
        wgt = sb("C_wgt", (128, 8))
        wgtba = sb("C_wgtba", (128, 7 * HALF))
        wgtT = sb("C_wgtT", (8, HALF))
        sel8 = sb("C_sel8", (8, 8 * 128))
        id32 = sb("C_id32", (128, 128))
        rbias_b = sb("C_rbb", (128, 8))
        mub = sb("C_mu", (128, FT))
        rsb = sb("C_rsb", (128, FT))
        scrC = sb("C_scr", (128, FT))
        rb32 = sb("C_rb32", (1, 8))
        p.dma("sp", rb32[0:1, 0:7], dr["p_rbias"][None, :], "s_Cw")
        p.dma("sp", sel8[:], dr["c_sel8"], "s_Cw")
        p.dma("sp", id32[:], dr["c_id32"], "s_Cw")
        wrt32 = sb("C_wrt32", (128, 8 * 7))
        p.dma("sp", wrt32[:].rearrange("p (k l) -> p k l", l=7),
              dr["w_router"].rearrange("k p l -> p k l"), "s_Cw")
        wC = p.tok("s_Cw")
        tms = p.emit("dve", lambda e: e.memset(rb32[0:1, 7:8], 0.0), waits=[wC],
                     inc=("dve", 1))
        tcw = p.emit("dve", lambda e: e.tensor_copy(wrt[:], wrt32[:]),
                     inc=("dve", 1))
        tbb = p.emit("pe", lambda e: e.matmul(ps_r[:, 0:8], self.ones_f[0:1, :],
                                              rb32[0:1, 0:8], start=True, stop=True),
                     waits=[wC, self.ones_tok, tms], inc=("pe", 1))
        tbc = p.emit("dve", lambda e: e.tensor_copy(rbias_b[:], ps_r[:, 0:8]),
                     waits=[tbb], inc=("dve", 1))

        gfree = [None, None]
        ufree = [None, None]
        dfree = [None, None]
        accfree = None
        outst = [None]
        psg_free = [None, None]
        psd_free = [None, None]
        tpb_free = None
        sgs_free = None
        mb_last = [None, None]
        psr_free = [tbc]
        psbc2_free = [None]
        ttrq = [None]
        bcast_last = [None]

        def load_min(hf):
            par = hf % 2
            s = (hf * HALF) // T
            t0 = (hf * HALF) % T
            return p.dma("sp", minb[par][:].rearrange("p (c f) -> p c f", f=HALF),
                         self.dfm(self.min_d[s][:, :, t0:t0 + HALF]), "s_Cm",
                         waits=[mb_last[par]])

        def emit_router(hf, tmin):
            mb = minb[hf % 2]
            ttrs = []

            def topk_chain(tb, tlg, po):
                tsg = p.emit("act", lambda e, po=po: e.activation(
                    probs[:, po:po + 7], probs[:, po:po + 7], AF.Sigmoid),
                    waits=[tlg], inc=("act", 1))
                p.emit("dve", lambda e, po=po: e.reduce_max(
                    m1[:], probs[:, po:po + 7], axis=mybir.AxisListType.X),
                    waits=[tsg], inc=("dve", 1))
                p.emit("dve", lambda e, po=po: e.tensor_single_scalar(
                    msk[:, 0:7], probs[:, po:po + 7], m1[:], ALU.is_equal),
                    inc=("dve", 1))
                p.emit("dve", lambda e, po=po: e.scalar_tensor_tensor(
                    msk[:, 8:15], msk[:, 0:7], -1e9, probs[:, po:po + 7],
                    ALU.mult, ALU.add), inc=("dve", 1))
                p.emit("dve", lambda e: e.reduce_max(m2[:], msk[:, 8:15],
                                                     axis=mybir.AxisListType.X),
                       inc=("dve", 1))
                p.emit("dve", lambda e: e.tensor_single_scalar(
                    msk[:, 8:15], msk[:, 8:15], m2[:], ALU.is_equal), inc=("dve", 1))
                p.emit("dve", lambda e: e.tensor_add(wgt[:, 7:8], m1[:], m2[:]),
                       waits=[ttrq[0]], inc=("dve", 1))
                p.emit("dve", lambda e: e.reciprocal(wgt[:, 7:8], wgt[:, 7:8]),
                       inc=("dve", 1))
                p.emit("dve", lambda e: e.tensor_mul(m1[:], m1[:], wgt[:, 7:8]),
                       inc=("dve", 1))
                p.emit("dve", lambda e: e.tensor_mul(m2[:], m2[:], wgt[:, 7:8]),
                       inc=("dve", 1))
                p.emit("dve", lambda e: e.tensor_scalar_mul(
                    wgt[:, 0:7], msk[:, 0:7], m1[:]), inc=("dve", 1))
                tb_ = p.emit("dve", lambda e: e.scalar_tensor_tensor(
                    wgt[:, 0:7], msk[:, 8:15], m2[:], wgt[:, 0:7],
                    ALU.mult, ALU.add), inc=("dve", 1))
                ttr = p.emit("pe", lambda e, tb=tb: e.transpose(
                    self.ps_bc[0][0:8, tb * 128:(tb + 1) * 128], wgt[:, 0:8],
                    id32[:]),
                    waits=[tb_, wC] + ([self.psbc_free[0]] if tb == 0 else []),
                    inc=("pe", 1))
                ttrq[0] = ttr
                ttrs.append(ttr)

            rpend = None
            for tb in range(HALF // 128):
                po = (tb % 2) * 8
                tmm = None
                for kc in range(8):
                    tmm = p.emit("pe", lambda e, tb=tb, kc=kc, mb=mb: e.matmul(
                        ps_r[:, 0:7],
                        mb[:, kc * HALF + tb * 128: kc * HALF + tb * 128 + 128],
                        wrt[:, kc * 7:(kc + 1) * 7],
                        start=(kc == 0), stop=(kc == 7)),
                        waits=([tmin, tcw, psr_free[0]] if kc == 0 else ()),
                        inc=("pe", 1) if kc == 7 else None)
                tlg = p.emit("dve", lambda e, po=po: e.tensor_add(
                    probs[:, po:po + 7], ps_r[:, 0:7], rbias_b[:, 0:7]),
                    waits=[tmm, tbc], inc=("dve", 1))
                psr_free[0] = tlg
                if rpend is not None:
                    topk_chain(*rpend)
                rpend = (tb, tlg, po)
            topk_chain(*rpend)
            tcp = p.emit("dve", lambda e: e.tensor_copy(wgtT[:],
                                                        self.ps_bc[0][0:8, :]),
                         waits=[ttrs[-1], bcast_last[0]], inc=("dve", 1))
            twb = None
            bankfree = [tcp, psbc2_free[0]]
            for e_ in range(7):
                bank = self.ps_bc[0] if e_ % 2 == 0 else ps_bc2
                tbc2 = p.emit("pe", lambda e, e_=e_, bank=bank: e.matmul(
                    bank[:, 0:HALF], sel8[:, e_ * 128:(e_ + 1) * 128], wgtT[:],
                    start=True, stop=True),
                    waits=[tcp, wC, bankfree[e_ % 2]], inc=("pe", 1))
                twb = p.emit("dve", lambda e, e_=e_, bank=bank: e.tensor_copy(
                    wgtba[:, e_ * HALF:(e_ + 1) * HALF], bank[:, 0:HALF]),
                    waits=[tbc2], inc=("dve", 1))
                bankfree[e_ % 2] = twb
                if e_ % 2 == 0:
                    self.psbc_free[0] = twb
                else:
                    psbc2_free[0] = twb
                bcast_last[0] = tbc2
            return (tmin, twb)

        tmin0 = load_min(0)
        rstate = [emit_router(0, tmin0)]
        for hf in range(NH_):
            par = hf % 2
            mb = minb[par]
            s = (hf * HALF) // T
            t0 = (hf * HALF) % T
            ctmin, ctwb = rstate[0]
            tmin_next = load_min(hf + 1) if hf + 1 < NH_ else None
            trl = p.dma("sp", rbt[:].rearrange("p (c f) -> p c f", f=FT),
                        self.dfm(self.r_fm[s][:, :, t0:t0 + HALF]),
                        "s_Cr", waits=[outst[0]])
            glast = ulast = None
            for e_ in range(8):
                cd_g = self.cast_done["w_moe_g"] if hf == 0 and e_ == 0 else []
                cd_u = self.cast_done["w_moe_u"] if hf == 0 and e_ == 0 else []
                tgh = [None, None]
                tuh = [None, None]
                for half in range(2):
                    tgh[half] = p.dma(
                        "sp",
                        bg[:].rearrange("p (k l) -> p k l", l=2048)[:, :, half * 1024:(half + 1) * 1024],
                        self.wb["w_moe_g"][e_].rearrange("k p l -> p k l")[:, :, half * 1024:(half + 1) * 1024],
                        "s_Cg%d" % half, waits=[gfree[half]] + cd_g)
                    tuh[half] = p.dma(
                        "sp",
                        bu[:].rearrange("p (k l) -> p k l", l=2048)[:, :, half * 1024:(half + 1) * 1024],
                        self.wb["w_moe_u"][e_].rearrange("k p l -> p k l")[:, :, half * 1024:(half + 1) * 1024],
                        "s_Cu%d" % half, waits=[ufree[half]] + cd_u)
                td_ = [None, None]
                for dh in range(2):
                    td_[dh] = p.dma(
                        "sp", bd[dh][:].rearrange("p (k l) -> p k l", l=512),
                        self.wb["w_moe_d"][e_].rearrange("k p l -> p k l")[:, :, dh * 512:(dh + 1) * 512],
                        "s_Cd%d" % dh,
                        waits=[dfree[dh]] + (self.cast_done["w_moe_d"] if hf == 0 and e_ == 0 else []))
                g7 = u7 = None
                for tl in range(NTL):
                    tglast = None
                    for c in range(16):
                        gp = c % 2
                        tgm = tum = None
                        for kc in range(8):
                            tgm = p.emit("pe", lambda e, c=c, kc=kc, tl=tl, gp=gp, mb=mb: e.matmul(
                                ps_g[gp][:, 0:FT],
                                bg[:, kc * 2048 + c * 128: kc * 2048 + (c + 1) * 128],
                                mb[:, kc * HALF + tl * FT: kc * HALF + (tl + 1) * FT],
                                start=(kc == 0), stop=(kc == 7)),
                                waits=([tgh[c // 8], ctmin, psg_free[gp]] if kc == 0 else ()),
                                inc=("pe", 1) if kc == 7 else None)
                        if tl == NTL - 1 and c == 7:
                            g7 = tgm
                        if tl == NTL - 1 and c == 15:
                            glast = tgm
                        tsl = p.emit("act", lambda e, gp=gp: e.activation(
                            sgs[:], ps_g[gp][:, 0:FT], AF.Silu),
                            waits=[tgm, sgs_free], inc=("act", 1))
                        for kc in range(8):
                            tum = p.emit("pe", lambda e, c=c, kc=kc, tl=tl, gp=gp, mb=mb: e.matmul(
                                ps_d[gp][:, 0:FT],
                                bu[:, kc * 2048 + c * 128: kc * 2048 + (c + 1) * 128],
                                mb[:, kc * HALF + tl * FT: kc * HALF + (tl + 1) * FT],
                                start=(kc == 0), stop=(kc == 7)),
                                waits=([tuh[c // 8], psd_free[gp]] if kc == 0 else ()),
                                inc=("pe", 1) if kc == 7 else None)
                        if tl == NTL - 1 and c == 7:
                            u7 = tum
                        if tl == NTL - 1 and c == 15:
                            ulast = tum
                        tt_ = p.emit("dve", lambda e, c=c, gp=gp: e.tensor_mul(
                            tpb[:, c * FT:(c + 1) * FT], sgs[:], ps_d[gp][:, 0:FT]),
                            waits=[tsl, tum, tpb_free], inc=("dve", 1))
                        psg_free[gp] = tt_
                        sgs_free = tt_
                        if e_ < 7:
                            tt_ = p.emit("dve", lambda e, c=c, tl=tl, e_=e_: e.tensor_mul(
                                tpb[:, c * FT:(c + 1) * FT], tpb[:, c * FT:(c + 1) * FT],
                                wgtba[:, e_ * HALF + tl * FT: e_ * HALF + (tl + 1) * FT]),
                                waits=[ctwb], inc=("dve", 1))
                        psd_free[gp] = tt_
                        tglast = tt_
                    for d in range(8):
                        gp = d % 2
                        dh = d // 4
                        tdm = None
                        for c in range(16):
                            tdm = p.emit("pe", lambda e, d=d, c=c, gp=gp, dh=dh: e.matmul(
                                ps_g[gp][:, 0:FT],
                                bd[dh][:, c * 512 + (d % 4) * 128: c * 512 + (d % 4 + 1) * 128],
                                tpb[:, c * FT:(c + 1) * FT],
                                start=(c == 0), stop=(c == 15)),
                                waits=([td_[dh], tglast, psg_free[gp]] if c == 0 else ()),
                                inc=("pe", 1) if c == 15 else None)
                        if tl == NTL - 1 and d == 3:
                            dfree[0] = tdm
                        if tl == NTL - 1 and d == 7:
                            dfree[1] = tdm
                        if e_ == 0:
                            tac = p.emit("dve", lambda e, d=d, tl=tl, gp=gp: e.tensor_copy(
                                acc[:, d * HALF + tl * FT: d * HALF + (tl + 1) * FT],
                                ps_g[gp][:, 0:FT]),
                                waits=[tdm, accfree], inc=("dve", 1))
                        else:
                            tac = p.emit("dve", lambda e, d=d, tl=tl, gp=gp: e.tensor_add(
                                acc[:, d * HALF + tl * FT: d * HALF + (tl + 1) * FT],
                                acc[:, d * HALF + tl * FT: d * HALF + (tl + 1) * FT],
                                ps_g[gp][:, 0:FT]),
                                waits=[tdm], inc=("dve", 1))
                        psg_free[gp] = tac
                tpb_free = p.tok("pe")
                gfree = [g7, glast]
                ufree = [u7, ulast]
            mb_last[par] = ulast
            if hf + 1 < NH_:
                rstate[0] = emit_router(hf + 1, tmin_next)
            tacc = p.tok("dve")
            for tl in range(NTL):
                tsum = None
                for c in range(8):
                    tsum = p.emit("dve", lambda e, c=c, tl=tl: e.tensor_add(
                        rbt[:, c * FT:(c + 1) * FT], rbt[:, c * FT:(c + 1) * FT],
                        acc[:, c * HALF + tl * FT: c * HALF + (tl + 1) * FT]),
                        waits=([trl, tacc] if c == 0 else ()),
                        inc=("dve", 1))
                sqv = tpb[:, 0:8 * FT]
                self.sq_free = tpb_free
                tln = self.ln_fm(rbt, rbt, None, 0, FT, sqv, rows3, mub, rsb, scrC,
                                 tsum)
                to_ = p.dma("sp", self.dfm(self.out[s][:, :, t0 + tl * FT: t0 + (tl + 1) * FT]),
                            rbt[:].rearrange("p (c f) -> p c f", f=FT), "s_Cout",
                            waits=[tln])
                outst = [to_]
            accfree = p.tok("dve")
            tpb_free = p.tok("pe")

    def build(self):
        self.decl_dram()
        es = ExitStack()
        with es:
            self.es = es
            self.p = Prog(self.nc, es)
            self.psg_free = [None, None]
            self.psrow_free = [None, None]
            self.psbc_free = [None, None]
            self.sq_free = None
            self.guard = []

            self.stage_prep()
            if "A" in self.cfg.stages:
                with ExitStack() as ses:
                    old, self.es = self.es, ses
                    self.stage_A1()
                    self.es = old
                self.stage_scope()
            if "A2" in self.cfg.stages:
                with ExitStack() as ses:
                    old, self.es = self.es, ses
                    self.stage_A2()
                    self.es = old
                self.stage_scope()
            if "B1" in self.cfg.stages:
                with ExitStack() as ses:
                    old, self.es = self.es, ses
                    self.stage_B1()
                    self.es = old
                self.stage_scope()
            if "B2" in self.cfg.stages:
                with ExitStack() as ses:
                    old, self.es = self.es, ses
                    self.stage_B2()
                    self.es = old
                self.stage_scope()
            if "C" in self.cfg.stages:
                with ExitStack() as ses:
                    old, self.es = self.es, ses
                    self.stage_C()
                    self.es = old
                self.stage_scope()
            for s in list(self.p.sems):
                if s.startswith("s_"):
                    self.p.emit("sp", None, waits=[self.p.tok(s)])
            blk = es.enter_context(self.nc.Block())
            self.p.playback(blk)
        return self.nc


def build(cfg):
    return Builder(cfg).build()


def kernel(**inputs):
    cfg = Cfg()
    nc = build(cfg)
    in_maps = [host_prepare(inputs, core, cfg.S, cfg.T) for core in range(NCORE)]
    res = run_bass_kernel_spmd(nc, in_maps, core_ids=list(range(NCORE)))
    outs = []
    for core in range(NCORE):
        o = np.asarray(res.results[core]["out"])
        o = o.reshape(cfg.S, 1024, cfg.T).transpose(0, 2, 1)
        outs.append(o)
    return np.concatenate(outs, 0).astype(np.float32)
